# version 07
# baseline (speedup 1.0000x reference)
import numpy as np
from collections import defaultdict

import concourse.bass as bass
import concourse.mybir as mybir
from concourse.bass_utils import run_bass_kernel_spmd

F32 = mybir.dt.float32
BF16 = mybir.dt.bfloat16
I32 = mybir.dt.int32
AF = mybir.ActivationFunctionType
AL = mybir.AluOpType

D = 1024
NCK = 8
TOK = 4096
ST = 512
NST = TOK // ST
NSUB = ST // 128
H = 3072
NHC = H // 128
LR = 0.01
WD = 0.01
EPS = 1e-08
LN_EPS = 1e-5
ALRF = -(2.0 / D) * 0.1

ENGS = ("tensor", "vector", "scalar", "gpsimd", "sync")
ESEM = {"tensor": "T", "vector": "V", "scalar": "S", "gpsimd": "G", "sync": "Y"}
NQROT = 6
QSEM = {"gpsimd": "DG", "sync": "DY"}
SEMNAMES = (["T", "V", "S", "G", "Y", "CC"]
            + [f"DG{i}" for i in range(NQROT)] + [f"DY{i}" for i in range(NQROT)])


class Op:
    __slots__ = ("eng", "fn", "deps", "sem", "inc", "marked", "val", "drain")

    def __init__(self, eng, fn, deps, sem, inc):
        self.eng, self.fn, self.deps, self.sem, self.inc = eng, fn, deps, sem, inc
        self.marked = False
        self.val = None
        self.drain = False


class Buf:
    __slots__ = ("name", "w", "rs")

    def __init__(self, name):
        self.name = name
        self.w = []
        self.rs = []

    def alldeps(self):
        return list(self.w) + list(self.rs)


class Sched:
    def __init__(self):
        self.ops = {e: [] for e in ENGS}
        self.all = []
        self.qrot = {"gpsimd": 0, "sync": 0}

    def add(self, eng, fn, reads=(), writes=(), extra=(), dma=False, drain=False):
        deps = []
        for b in reads:
            deps.extend(b.w)
        for b in writes:
            deps.extend(b.w)
            deps.extend(b.rs)
        deps.extend(extra)
        if dma:
            sem = QSEM[eng] + str(self.qrot[eng] % NQROT)
            self.qrot[eng] += 1
        else:
            sem = ESEM[eng]
        op = Op(eng, fn, deps, sem, 16 if dma else 1)
        op.drain = drain
        self.ops[eng].append(op)
        self.all.append(op)
        for b in reads:
            b.rs.append(op)
        for b in writes:
            b.w = [op]
            b.rs = []
        if dma:
            op.marked = True
        return op

    def realize(self, nc, sems):
        for op in self.all:
            if op.drain:
                op.marked = True
            for d in op.deps:
                if not (d.eng == op.eng and d.sem == ESEM[op.eng]):
                    d.marked = True
        cnt = defaultdict(int)
        for e in ENGS:
            for op in self.ops[e]:
                if op.marked:
                    cnt[op.sem] += op.inc
                    op.val = cnt[op.sem]

        with nc.Block() as block:
            def runner(eng):
                def go(e):
                    floor = defaultdict(int)
                    for op in self.ops[eng]:
                        want = {}
                        for d in op.deps:
                            if d.eng == eng and d.sem == ESEM[eng]:
                                continue
                            if want.get(d.sem, 0) < d.val:
                                want[d.sem] = d.val
                        for s, v in sorted(want.items()):
                            if v > floor[s]:
                                e.wait_ge(sems[s], v)
                                floor[s] = v
                        ins = op.fn(e)
                        if op.marked:
                            ins.then_inc(sems[op.sem], op.inc)
                        if op.drain:
                            e.wait_ge(sems[op.sem], op.val)
                            floor[op.sem] = max(floor[op.sem], op.val)
                return go
            for eng in ENGS:
                getattr(block, eng)(runner(eng))


def build_nc(taps=(), variant=""):
    nc = bass.Bass("TRN2", target_bir_lowering=False, debug=False)
    sc = Sched()

    x_d = nc.dram_tensor("L0_attn_x", [TOK, D], F32, kind="ExternalInput").ap()
    W_d = {}
    for n in ("Wk", "Wq", "Wv", "W1", "W2"):
        W_d[n] = nc.dram_tensor("L0_attn_" + n, [D, D], F32, kind="ExternalInput").ap()
    b_d = {}
    for n in ("bk", "bq", "bv", "b1", "b2"):
        b_d[n] = nc.dram_tensor("L0_attn_" + n, [D], F32, kind="ExternalInput").ap()
    Wlr_d = nc.dram_tensor("L0_attn_Wlr", [D, 1], F32, kind="ExternalInput").ap()
    blr_d = nc.dram_tensor("L0_attn_blr", [1], F32, kind="ExternalInput").ap()
    wg_d = nc.dram_tensor("L0_ffn_w_gate", [H, D], I32, kind="ExternalInput").ap()
    wu_d = nc.dram_tensor("L0_ffn_w_up", [H, D], I32, kind="ExternalInput").ap()
    wd_d = nc.dram_tensor("L0_ffn_w_down", [D, H], I32, kind="ExternalInput").ap()
    out_d = nc.dram_tensor("out", [TOK, D], F32, kind="ExternalOutput").ap()

    if "noint" in variant:
        q_d = xp_d = None
    else:
        q_d = nc.dram_tensor("q_spill", [NST * 128, NCK * ST], BF16, kind="Internal").ap()
        xp_d = nc.dram_tensor("xp_spill", [TOK, D], BF16, kind="Internal").ap()
    CCW = 16 * D + 16
    cc_in = nc.dram_tensor("cc_in", [128, CCW], F32, kind="Internal").ap()
    cc_out = nc.dram_tensor("cc_out", [128, CCW], F32, kind="Internal",
                            addr_space="Shared").ap()
    if "noint" in variant:
        wgT_d = wuT_d = wdT_d = None
    else:
        wgT_d = nc.dram_tensor("wgT_d", [128, NCK * H], BF16, kind="Internal").ap()
        wuT_d = nc.dram_tensor("wuT_d", [128, NCK * H], BF16, kind="Internal").ap()
        wdT_d = nc.dram_tensor("wdT_d", [128, NHC * D], BF16, kind="Internal").ap()

    tap_d = {}
    for tname, shape, dt in taps:
        tap_d[tname] = nc.dram_tensor("tap_" + tname, shape, dt,
                                      kind="ExternalOutput").ap()

    OFF = {}
    pos = 0

    def place(name, nbytes):
        nonlocal pos
        OFF[name] = pos
        pos += (nbytes + 31) & ~31

    place("R1", 98304)
    place("R2", 32768)
    place("RW", 16384)
    place("R3", 57344)
    place("SMALL", 4096)
    arena = nc.alloc_sbuf_tensor("arena", [128, pos], mybir.dt.uint8)
    base = nc.lookup_mloc(arena).addr

    _atcnt = [0]

    def sb(name, shape, dtype, region, roff):
        if "noat" in variant:
            _atcnt[0] += 1
            if _atcnt[0] > 1:
                return None
            return nc.alloc_sbuf_tensor_at(name, shape, dtype,
                                           offset=base + OFF["R5"]).ap()
        return nc.alloc_sbuf_tensor_at(name, shape, dtype,
                                       offset=base + OFF[region] + roff).ap()

    soff = [0]

    def small(name, shape, dtype):
        t = sb(name, shape, dtype, "SMALL", soff[0])
        soff[0] += (int(np.prod(shape[1:])) * mybir.dt.size(dtype) + 31) & ~31
        return t

    ident = small("ident", [128, 128], BF16)
    ones_col = small("ones_col", [128, 1], BF16)
    ones_row = small("ones_row", [1, 128], F32)
    blr_bc = small("blr_bc", [128, 1], F32)
    bias_sb = {n: small("c_" + n, [128, NCK], F32)
               for n in ("bk", "bq", "bv", "b1", "b2", "nb1", "nb2")}
    wlr_sb = small("wlr", [128, NCK], BF16)
    stat = small("stat", [128, 512], F32)
    alrS = small("alrS", [128, 64], F32)
    gA = small("gA", [128, NCK], F32)
    gB = small("gB", [128, NCK], F32)
    alrneg = small("alrneg", [128, NSUB], F32)
    assert soff[0] <= 4096, soff[0]

    WNAMES = ("Wk", "Wq", "Wv", "W1", "W2", "W2T")
    Wsb = {n: sb(n, [128, NCK * D], BF16, "R1", i * 16384)
           for i, n in enumerate(WNAMES)}
    ring = {n: sb(n, [128, NSUB * D], BF16, "R2", i * 8192)
            for i, n in enumerate(("rk", "rh1", "rdz1", "rdz2"))}
    stgA = sb("stgA", [128, D], F32, "RW", 0)
    stgB = sb("stgB", [128, D], F32, "RW", 4096)
    POOL_N = 7
    pool = [sb(f"pool{i}", [128, NCK * ST], BF16, "R3", i * 8192)
            for i in range(POOL_N)]
    nW1 = sb("nW1", [128, NCK * D], BF16, "R3", 0)
    nW2 = sb("nW2", [128, NCK * D], BF16, "R3", 16384)
    bQ = sb("bQ", [128, NCK * ST], BF16, "R3", 32768)
    bTA = sb("bTA", [128, NCK * ST], BF16, "R3", 40960)
    bQH1 = sb("bQH1", [128, NCK * ST], BF16, "R3", 49152)
    bRT = sb("bRT", [128, NSUB * D], BF16, "RW", 0)
    stgP = sb("stgP", [128, D], F32, "RW", 0)
    stgPb = sb("stgPb", [128, D], F32, "RW", 4096)
    stgW = [sb(f"stgW{i}", [128, D], BF16, "RW", 8192 + i * 2048) for i in range(4)]
    wgT = sb("wgT", [128, NCK * H], BF16, "R1", 0)
    wuT = sb("wuT", [128, NCK * H], BF16, "R1", 49152)
    wdTa = sb("wdTa", [128, 16 * D], BF16, "R2", 0)
    wdTb = sb("wdTb", [128, 8 * D], BF16, "RW", 0)
    fXP = sb("fXP", [128, NSUB * D], BF16, "R3", 0)
    fLN_tm = sb("fLN_tm", [128, NSUB * D], BF16, "R3", 8192)
    fLN_fm = sb("fLN_fm", [128, NCK * ST], BF16, "R3", 16384)
    fOUT = sb("fOUT", [128, NSUB * D], BF16, "R3", 24576)
    fH = sb("fH", [128, NHC * ST], BF16, "R3", 32768)
    fFIN = sb("fFIN", [128, NSUB * D], F32, "R3", 32768)

    if "nopsum" in variant:
        psM = psT = psG = [None, None]
    else:
        psM = [nc.alloc_psum_tensor("psM0", [128, 512], F32).ap(),
               nc.alloc_psum_tensor("psM1", [128, 512], F32).ap()]
        psT = [nc.alloc_psum_tensor("psT0", [128, 1024], BF16).ap(),
               nc.alloc_psum_tensor("psT1", [128, 1024], BF16).ap()]
        psG = [nc.alloc_psum_tensor("psG0", [128, 1024], F32).ap(),
               nc.alloc_psum_tensor("psG1", [128, 1024], F32).ap()]

    _B = {}

    def buf(name):
        if name not in _B:
            _B[name] = Buf(name)
        return _B[name]

    bufW = {n: buf("W_" + n) for n in WNAMES}
    bufRing = {n: buf(n) for n in ring}
    bufPool = [buf(f"pool{i}") for i in range(POOL_N)]
    bufStgA, bufStgB = buf("stgA"), buf("stgB")
    bufPsM = [buf("psM0"), buf("psM1")]
    bufPsT = [buf("psT0"), buf("psT1")]
    bufPsGh = [buf("psG0a"), buf("psG0b"), buf("psG1a"), buf("psG1b")]
    bufStat = buf("stat")
    bufAlr = buf("alrneg")
    bufSmall = buf("small")
    bufBias = {n: buf("bias_" + n) for n in bias_sb}
    bufQd = [buf(f"q_d{st}") for st in range(NST)]
    bufXpd = [buf(f"xp_d{st}") for st in range(NST)]
    bufG = {(m, c): buf(f"g{m}_{c}") for m in range(2) for c in range(NCK)}
    bufGb = buf("gbias")
    bufCCout = buf("cc_out")
    bufWT_d = {n: buf(n + "_dram") for n in ("wgT", "wuT", "wdT")}
    bufFW = {n: buf("sb_" + n) for n in ("wgT", "wuT", "wdTa", "wdTb")}

    def wd_ap(hc, hf):
        if hc < 16:
            return wdTa[:, hc * D + hf * 512: hc * D + hf * 512 + 512]
        return wdTb[:, (hc - 16) * D + hf * 512: (hc - 16) * D + hf * 512 + 512]
    bufStgP, bufStgPb = buf("stgP"), buf("stgPb")
    bufStgW = [buf(f"stgW{i}") for i in range(4)]
    bnW = {"W1": buf("nW1"), "W2": buf("nW2")}
    bufQ, bufTA, bufQH1 = buf("bQ"), buf("bTA"), buf("bQH1")
    bufRT = buf("bRT")
    bufFXP, bufFLNtm, bufFLNfm = buf("fXP"), buf("fLN_tm"), buf("fLN_fm")
    bufFOUT, bufFH, bufFFIN = buf("fOUT"), buf("fH"), buf("fFIN")

    rot = {"M": [0], "T": [0]}
    pool_full = [(psM[0], bufPsM[0]), (psM[1], bufPsM[1]),
                 (psG[0][:, 0:512], bufPsGh[0]), (psG[0][:, 512:1024], bufPsGh[1]),
                 (psG[1][:, 0:512], bufPsGh[2]), (psG[1][:, 512:1024], bufPsGh[3])]
    pool_mm = [pool_full]

    def next_ps(kind):
        if kind == "T":
            i = rot["T"][0] % 2
            rot["T"][0] += 1
            return psT[i], bufPsT[i]
        p = pool_mm[0]
        i = rot["M"][0] % len(p)
        rot["M"][0] += 1
        return p[i]

    def mm_group(items, reads, wpsum, extra=()):
        wps = wpsum if isinstance(wpsum, tuple) else (wpsum,)
        n = len(items)
        war = list(extra)
        for w in wps:
            war.extend(w.alldeps())
        for j, (o, l, r, st_, sp_) in enumerate(items):
            def f(e, o=o, l=l, r=r, st_=st_, sp_=sp_):
                return e.matmul(o, l, r, start=st_, stop=sp_)
            sc.add("tensor", f,
                   reads=reads if (j == 0 or j == n - 1) else (),
                   writes=wps if j == n - 1 else (),
                   extra=war if j == 0 else ())

    def trans8(src_fn, ps, bps, reads):
        for c in range(NCK):
            def f(e, c=c):
                return e.transpose(ps[:, c * 128:(c + 1) * 128], src_fn(c), ident[:, :])
            sc.add("tensor", f,
                   reads=reads if (c == 0 or c == NCK - 1) else (),
                   writes=(bps,) if c == NCK - 1 else (),
                   extra=(bps.alldeps() + list(bufSmall.w)) if c == 0 else ())

    def act(out, in_, func, bias=0.0, scale=1.0, accum=None,
            reads=(), writes=(), extra=(), drain=False):
        def f(e):
            kw = {"accum_out": accum} if accum is not None else {}
            return e.activation(out, in_, func, bias=bias, scale=scale, **kw)
        sc.add("scalar", f, reads=reads, writes=writes, extra=extra, drain=drain)

    def vts(out, in0, s1, s2, op0, op1=None, reads=(), writes=(), extra=(),
            drain=False):
        def f(e):
            if s2 is None:
                return e.tensor_scalar(out=out, in0=in0, scalar1=s1, scalar2=None,
                                       op0=op0)
            return e.tensor_scalar(out=out, in0=in0, scalar1=s1, scalar2=s2,
                                   op0=op0, op1=op1)
        sc.add("vector", f, reads=reads, writes=writes, extra=extra, drain=drain)

    def vtt(out, a, b_, op, reads=(), writes=(), extra=(), drain=False):
        def f(e):
            return e.tensor_tensor(out=out, in0=a, in1=b_, op=op)
        sc.add("vector", f, reads=reads, writes=writes, extra=extra, drain=drain)

    def vstt(out, in0, scalar, in1, op0, op1, reads=(), writes=(), extra=()):
        def f(e):
            return e.scalar_tensor_tensor(out=out, in0=in0, scalar=scalar, in1=in1,
                                          op0=op0, op1=op1)
        sc.add("vector", f, reads=reads, writes=writes, extra=extra)

    def vcopy(out, in_, reads=(), writes=(), extra=(), drain=False):
        sc.add("vector", lambda e: e.tensor_copy(out, in_),
               reads=reads, writes=writes, extra=extra, drain=drain)

    def scopy(out, in_, reads=(), writes=(), extra=(), drain=False):
        sc.add("scalar", lambda e: e.copy(out, in_),
               reads=reads, writes=writes, extra=extra, drain=drain)

    def vrecip(out, in_, reads=(), writes=(), drain=False):
        sc.add("vector", lambda e: e.reciprocal(out, in_), reads=reads,
               writes=writes, drain=drain)

    def dma(q, out, in_, reads=(), writes=(), extra=(), **kw):
        return sc.add(q, lambda e: e.dma_start(out, in_, **kw),
                      reads=reads, writes=writes, extra=extra, dma=True)

    def ln_stats(src, bsrc, scr, bscr, extra0=()):
        for s in range(NSUB):
            act(scr[:, s * D:(s + 1) * D], src[:, s * D:(s + 1) * D], AF.Identity,
                accum=stat[:, s:s + 1], reads=(bsrc,), writes=(bscr, bufStat),
                extra=extra0 if s == 0 else ())
            act(scr[:, s * D:(s + 1) * D], src[:, s * D:(s + 1) * D], AF.Square,
                accum=stat[:, 4 + s:5 + s], reads=(bsrc,), writes=(bscr, bufStat))
        ln_minimath()

    def ln_minimath():
        vts(stat[:, 64:128], stat[:, 0:64], -1.0 / D, None, AL.mult,
            reads=(bufStat,), writes=(bufStat,))
        vtt(stat[:, 128:192], stat[:, 64:128], stat[:, 64:128], AL.mult,
            reads=(bufStat,), writes=(bufStat,))
        vtt(stat[:, 192:256], stat[:, 68:132], stat[:, 128:192], AL.add,
            reads=(bufStat,), writes=(bufStat,))
        vts(stat[:, 256:320], stat[:, 192:256], -1.0, LN_EPS, AL.mult, AL.add,
            reads=(bufStat,), writes=(bufStat,))
        act(stat[:, 320:384], stat[:, 256:320], AF.Sqrt, reads=(bufStat,),
            writes=(bufStat,))
        vrecip(stat[:, 384:448], stat[:, 320:384], reads=(bufStat,),
               writes=(bufStat,))
        vtt(stat[:, 448:512], stat[:, 64:128], stat[:, 384:448], AL.mult,
            reads=(bufStat,), writes=(bufStat,))

    skip = lambda tag: ("q" + tag) in variant
    if not skip("1"):
        sc.add("gpsimd", lambda e: e.memset(ident[:, :], 0.0), writes=(bufSmall,))

    def f_id(e):
        return e.affine_select(out=ident[:, :], in_=ident[:, :],
                               compare_op=AL.not_equal, fill=1.0, base=0,
                               pattern=[[-1, 128]], channel_multiplier=1)
    if not skip("1"):
        sc.add("gpsimd", f_id, reads=(bufSmall,), writes=(bufSmall,))
        sc.add("gpsimd", lambda e: e.memset(ones_col[:, :], 1.0), writes=(bufSmall,))
    if not skip("2"):
        for n in ("bk", "bq", "bv", "b1", "b2"):
            dma("sync", bias_sb[n][:, :], b_d[n].rearrange("(c p) -> p c", p=128),
                writes=(bufBias[n],), allow_slow_non_contiguous=True)
        dma("gpsimd", wlr_sb[:, :], Wlr_d.rearrange("(c p) o -> p (c o)", p=128),
            writes=(bufSmall,), allow_slow_non_contiguous=True)
        dma("sync", blr_bc[:1, :], blr_d[None, :], writes=(bufSmall,))
        sc.add("gpsimd", lambda e: e.memset(ones_row[:, :], 1.0), writes=(bufSmall,))
        _psb, _bpsb = next_ps("M")
        sc.add("tensor",
               lambda e: e.matmul(_psb[:, 0:1], ones_row[:, :], blr_bc[:1, 0:1],
                                  start=True, stop=True),
               reads=(bufSmall,), writes=(_bpsb,), extra=_bpsb.alldeps())
        scopy(blr_bc[:, :], _psb[:, 0:1], reads=(_bpsb,), writes=(bufSmall,),
              drain=True)
    if not skip("3"):
        for n in ("Wk", "Wq", "Wv", "W1", "W2"):
            for kc in range(NCK):
                dma("gpsimd", Wsb[n][:, kc * D:(kc + 1) * D],
                    W_d[n][kc * 128:(kc + 1) * 128, :],
                    writes=(bufW[n],) if kc == NCK - 1 else (),
                    extra=bufW[n].alldeps() if kc == 0 else ())
    if not skip("4"):
        for kc in range(NCK):
            ps, bps = next_ps("T")
            trans8(lambda ic, kc=kc: Wsb["W2"][:, ic * D + kc * 128: ic * D + kc * 128 + 128],
                   ps, bps, reads=(bufW["W2"],))
            scopy(Wsb["W2T"][:, kc * D:(kc + 1) * D], ps[:, :], reads=(bps,),
                  writes=(bufW["W2T"],) if kc == NCK - 1 else (),
                  extra=bufW["W2T"].alldeps() if kc == 0 else ())

    free_slots = list(range(POOL_N))
    slot_of = {}

    def palloc(name):
        i = free_slots.pop(0)
        slot_of[name] = i
        return pool[i], bufPool[i]

    def pfree(name):
        free_slots.append(slot_of.pop(name))

    def prename(new, old):
        slot_of[new] = slot_of.pop(old)

    grad_first = [True]

    def emit_stA(st):
        X, bX_ = palloc("X")
        for s in range(NSUB):
            dma("gpsimd", X[:, s * D:(s + 1) * D],
                x_d[st * ST + s * 128: st * ST + (s + 1) * 128, :],
                writes=(bX_,) if s == NSUB - 1 else (),
                extra=bX_.alldeps() if s == 0 else ())
        if ("x%d" % st) in tap_d:
            dma("sync", tap_d["x%d" % st][:, :], X[:, :], reads=(bX_,),
                writes=(buf("tap_x%d" % st),))
        SCR, bSCR = palloc("SCR")
        ln_stats(X, bX_, SCR, bSCR)
        pfree("SCR")
        LNX, bLNX = palloc("LNX")
        for s in range(NSUB):
            vts(X[:, s * D:(s + 1) * D], X[:, s * D:(s + 1) * D],
                stat[:, 384 + s:385 + s], stat[:, 448 + s:449 + s], AL.mult, AL.add,
                reads=(bX_, bufStat), writes=(bX_,))
            ps, bps = next_ps("T")
            trans8(lambda c, s=s: X[:, s * D + c * 128: s * D + (c + 1) * 128],
                   ps, bps, reads=(bX_,))
            scopy(LNX.rearrange("p (c t) -> p c t", t=ST)[:, :, s * 128:(s + 1) * 128],
                  ps.rearrange("p (c t) -> p c t", t=128),
                  reads=(bps,), writes=(bLNX,) if s == NSUB - 1 else (),
                  extra=bLNX.alldeps() if s == 0 else ())
        pfree("X")

        def proj(wn, biasn, OUTT, bOUT, rhs, brhs):
            for oc in range(NCK):
                ps, bps = next_ps("M")
                mm_group([(ps[:, :],
                           Wsb[wn][:, kc * D + oc * 128: kc * D + (oc + 1) * 128],
                           rhs[:, kc * ST:(kc + 1) * ST], kc == 0, kc == NCK - 1)
                          for kc in range(NCK)],
                         reads=(bufW[wn], brhs), wpsum=bps)
                act(OUTT[:, oc * ST:(oc + 1) * ST], ps[:, :], AF.Identity,
                    bias=bias_sb[biasn][:, oc:oc + 1],
                    reads=(bps, bufBias[biasn]),
                    writes=(bOUT,) if oc == NCK - 1 else (),
                    extra=bOUT.alldeps() if oc == 0 else ())

        if st == 0 and "lnx" in tap_d:
            dma("sync", tap_d["lnx"][:, :], LNX[:, :], reads=(bLNX,),
                writes=(buf("tap_lnx"),))
        if ("lnx%d" % st) in tap_d:
            dma("sync", tap_d["lnx%d" % st][:, :], LNX[:, :], reads=(bLNX,),
                writes=(buf("tap_lnx%d" % st),))
        if st == 0 and "stats" in tap_d:
            dma("sync", tap_d["stats"][:, :], stat[:, :], reads=(bufStat,),
                writes=(buf("tap_stats"),))
        if ("bk%d" % st) in tap_d:
            dma("sync", tap_d["bk%d" % st][:, :], bias_sb["bk"][:, :],
                reads=(bufBias["bk"],), writes=(buf("tap_bk%d" % st),))
        if ("wk%d" % st) in tap_d:
            dma("sync", tap_d["wk%d" % st][:, :], Wsb["Wk"][:, :],
                reads=(bufW["Wk"],), writes=(buf("tap_wk%d" % st),))
        K, bK = palloc("K")
        proj("Wk", "bk", K, bK, LNX, bLNX)
        if st == 0 and "kfm" in tap_d:
            dma("sync", tap_d["kfm"][:, :], K[:, :], reads=(bK,),
                writes=(buf("tap_kfm"),))
        if ("kfm%d" % st) in tap_d:
            dma("sync", tap_d["kfm%d" % st][:, :], K[:, :], reads=(bK,),
                writes=(buf("tap_kfm%d" % st),))
        Q, bQ_ = palloc("Q")
        proj("Wq", "bq", Q, bQ_, LNX, bLNX)
        dma("sync", q_d[st * 128:(st + 1) * 128, :], Q[:, :],
            reads=(bQ_,), writes=(bufQd[st],))
        pfree("Q")
        bAlrS = buf("alrS")
        for s in range(NSUB):
            ps, bps = next_ps("M")
            mm_group([(ps[:, 0:1],
                       LNX[:, kc * ST + s * 128: kc * ST + (s + 1) * 128],
                       wlr_sb[:, kc:kc + 1], kc == 0, kc == NCK - 1)
                      for kc in range(NCK)],
                     reads=(bufSmall, bLNX), wpsum=bps)
            act(alrS[:, s:s + 1], ps[:, 0:1], AF.Sigmoid, bias=blr_bc[:, 0:1],
                reads=(bps, bufSmall), writes=(bAlrS,),
                extra=bAlrS.alldeps() if s == 0 else ())
        def falr(e):
            return e.tensor_scalar(out=alrneg[:, 0:NSUB], in0=alrS[:, 0:NSUB],
                                   scalar1=ALRF, scalar2=None, op0=AL.mult)
        sc.add("vector", falr, reads=(bAlrS,), writes=(bufAlr,), drain=True)
        Z1, bZ1 = palloc("Z1")
        S1, bS1 = palloc("S1")
        for oc in range(NCK):
            ps, bps = next_ps("M")
            mm_group([(ps[:, :],
                       Wsb["W1"][:, kc * D + oc * 128: kc * D + (oc + 1) * 128],
                       K[:, kc * ST:(kc + 1) * ST], kc == 0, kc == NCK - 1)
                      for kc in range(NCK)],
                     reads=(bufW["W1"], bK), wpsum=bps)
            act(Z1[:, oc * ST:(oc + 1) * ST], ps[:, :], AF.Identity,
                bias=bias_sb["b1"][:, oc:oc + 1], reads=(bps, bufBias["b1"]),
                writes=(bZ1,) if oc == NCK - 1 else (),
                extra=bZ1.alldeps() if oc == 0 else ())
            act(S1[:, oc * ST:(oc + 1) * ST], ps[:, :], AF.Sigmoid,
                bias=bias_sb["b1"][:, oc:oc + 1], reads=(bps, bufBias["b1"]),
                writes=(bS1,) if oc == NCK - 1 else (),
                extra=bS1.alldeps() if oc == 0 else ())
        A1, bA1 = palloc("A1")
        vtt(A1[:, :], Z1[:, :], S1[:, :], AL.mult, reads=(bZ1, bS1), writes=(bA1,))
        vtt(Z1[:, :], A1[:, :], S1[:, :], AL.mult, reads=(bA1, bS1), writes=(bZ1,))
        Wt, bWt = palloc("Wt")
        vtt(Wt[:, :], S1[:, :], A1[:, :], AL.add, reads=(bS1, bA1), writes=(bWt,))
        vstt(S1[:, :], Z1[:, :], -1.0, Wt[:, :], AL.mult, AL.add,
             reads=(bZ1, bWt), writes=(bS1,))
        SP1, bSP1 = S1, bS1
        prename("SP1", "S1")
        vtt(Wt[:, :], K[:, :], A1[:, :], AL.add, reads=(bK, bA1), writes=(bWt,))
        H1, bH1 = Wt, bWt
        prename("H1", "Wt")
        pfree("A1")
        for s in range(NSUB):
            ps, bps = next_ps("T")
            trans8(lambda c, s=s: K[:, c * ST + s * 128: c * ST + (s + 1) * 128],
                   ps, bps, reads=(bK,))
            scopy(ring["rk"][:, s * D:(s + 1) * D], ps[:, :], reads=(bps,),
                  writes=(bufRing["rk"],) if s == NSUB - 1 else (),
                  extra=bufRing["rk"].alldeps() if s == 0 else ())
        pfree("K")
        Z2, bZ2 = Z1, bZ1
        S2, bS2 = palloc("S2")
        for oc in range(NCK):
            ps, bps = next_ps("M")
            mm_group([(ps[:, :],
                       Wsb["W2"][:, kc * D + oc * 128: kc * D + (oc + 1) * 128],
                       H1[:, kc * ST:(kc + 1) * ST], kc == 0, kc == NCK - 1)
                      for kc in range(NCK)],
                     reads=(bufW["W2"], bH1), wpsum=bps)
            act(Z2[:, oc * ST:(oc + 1) * ST], ps[:, :], AF.Identity,
                bias=bias_sb["b2"][:, oc:oc + 1], reads=(bps, bufBias["b2"]),
                writes=(bZ2,) if oc == NCK - 1 else (),
                extra=bZ2.alldeps() if oc == 0 else ())
            act(S2[:, oc * ST:(oc + 1) * ST], ps[:, :], AF.Sigmoid,
                bias=bias_sb["b2"][:, oc:oc + 1], reads=(bps, bufBias["b2"]),
                writes=(bS2,) if oc == NCK - 1 else (),
                extra=bS2.alldeps() if oc == 0 else ())
        A2, bA2 = palloc("A2")
        vtt(A2[:, :], Z2[:, :], S2[:, :], AL.mult, reads=(bZ2, bS2), writes=(bA2,))
        vtt(Z2[:, :], A2[:, :], S2[:, :], AL.mult, reads=(bA2, bS2), writes=(bZ2,))
        W2t, bW2t = palloc("W2t")
        vtt(W2t[:, :], S2[:, :], A2[:, :], AL.add, reads=(bS2, bA2), writes=(bW2t,))
        vstt(S2[:, :], Z2[:, :], -1.0, W2t[:, :], AL.mult, AL.add,
             reads=(bZ2, bW2t), writes=(bS2,))
        SP2, bSP2 = S2, bS2
        prename("SP2", "S2")
        pfree("Z1")
        vtt(W2t[:, :], H1[:, :], A2[:, :], AL.add, reads=(bH1, bA2), writes=(bW2t,))
        PRED, bPRED = W2t, bW2t
        prename("PRED", "W2t")
        for s in range(NSUB):
            ps, bps = next_ps("T")
            trans8(lambda c, s=s: H1[:, c * ST + s * 128: c * ST + (s + 1) * 128],
                   ps, bps, reads=(bH1,))
            scopy(ring["rh1"][:, s * D:(s + 1) * D], ps[:, :], reads=(bps,),
                  writes=(bufRing["rh1"],) if s == NSUB - 1 else (),
                  extra=bufRing["rh1"].alldeps() if s == 0 else ())
        pfree("H1")
        RNEG, bRNEG = A2, bA2
        prename("RNEG", "A2")
        for oc in range(NCK):
            ps, bps = next_ps("M")
            mm_group([(ps[:, :],
                       Wsb["Wv"][:, kc * D + oc * 128: kc * D + (oc + 1) * 128],
                       LNX[:, kc * ST:(kc + 1) * ST], kc == 0, kc == NCK - 1)
                      for kc in range(NCK)],
                     reads=(bufW["Wv"], bLNX), wpsum=bps)
            vstt(RNEG[:, oc * ST:(oc + 1) * ST], ps[:, :],
                 bias_sb["bv"][:, oc:oc + 1], PRED[:, oc * ST:(oc + 1) * ST],
                 AL.add, AL.subtract,
                 reads=(bps, bufBias["bv"], bPRED),
                 writes=(bRNEG,) if oc == NCK - 1 else (),
                 extra=bRNEG.alldeps() if oc == 0 else ())
        if ("lnxlate%d" % st) in tap_d:
            dma("sync", tap_d["lnxlate%d" % st][:, :], LNX[:, :], reads=(bLNX,),
                writes=(buf("tap_lnxlate%d" % st),))
        pfree("LNX")
        pfree("PRED")
        E2, bE2 = palloc("E2")
        vtt(E2[:, :], RNEG[:, :], SP2[:, :], AL.mult, reads=(bRNEG, bSP2), writes=(bE2,))
        pfree("SP2")
        for s in range(NSUB):
            ps, bps = next_ps("T")
            trans8(lambda c, s=s: E2[:, c * ST + s * 128: c * ST + (s + 1) * 128],
                   ps, bps, reads=(bE2,))
            vts(ring["rdz2"][:, s * D:(s + 1) * D], ps[:, :], alrneg[:, s:s + 1],
                None, AL.mult, reads=(bps, bufAlr),
                writes=(bufRing["rdz2"],) if s == NSUB - 1 else (),
                extra=bufRing["rdz2"].alldeps() if s == 0 else ())
        T1, bT1 = palloc("T1")
        for ic in range(NCK):
            ps, bps = next_ps("M")
            mm_group([(ps[:, :],
                       Wsb["W2T"][:, kc * D + ic * 128: kc * D + (ic + 1) * 128],
                       E2[:, kc * ST:(kc + 1) * ST], kc == 0, kc == NCK - 1)
                      for kc in range(NCK)],
                     reads=(bufW["W2T"], bE2), wpsum=bps)
            vstt(T1[:, ic * ST:(ic + 1) * ST], ps[:, :], 0.0,
                 RNEG[:, ic * ST:(ic + 1) * ST], AL.add, AL.add,
                 reads=(bps, bRNEG), writes=(bT1,) if ic == NCK - 1 else (),
                 extra=bT1.alldeps() if ic == 0 else ())
        pfree("E2")
        pfree("RNEG")
        D1, bD1 = palloc("D1")
        vtt(D1[:, :], T1[:, :], SP1[:, :], AL.mult, reads=(bT1, bSP1), writes=(bD1,))
        pfree("T1")
        pfree("SP1")
        for s in range(NSUB):
            ps, bps = next_ps("T")
            trans8(lambda c, s=s: D1[:, c * ST + s * 128: c * ST + (s + 1) * 128],
                   ps, bps, reads=(bD1,))
            vts(ring["rdz1"][:, s * D:(s + 1) * D], ps[:, :], alrneg[:, s:s + 1],
                None, AL.mult, reads=(bps, bufAlr),
                writes=(bufRing["rdz1"],) if s == NSUB - 1 else (),
                extra=bufRing["rdz1"].alldeps() if s == 0 else ())
        pfree("D1")
        for rn in ("rk", "rh1", "rdz1", "rdz2"):
            if rn + str(st) in tap_d:
                dma("sync", tap_d[rn + str(st)][:, :], ring[rn][:, :],
                    reads=(bufRing[rn],), writes=(buf("tap_%s%d" % (rn, st)),))
        if st == 0:
            for rn in ("rk", "rh1", "rdz1", "rdz2"):
                if rn in tap_d:
                    dma("sync", tap_d[rn][:, :], ring[rn][:, :],
                        reads=(bufRing[rn],), writes=(buf("tap_" + rn),))
            if "alr" in tap_d:
                dma("sync", tap_d["alr"][:, :], alrneg[:, :], reads=(bufAlr,),
                    writes=(buf("tap_alr"),))
        first = grad_first[0]
        for m, (lhs, dz) in enumerate((("rk", "rdz1"), ("rh1", "rdz2"))):
            for ic in range(NCK):
                gi = ic % 2
                gp = psG[gi]
                bgp = (bufPsGh[2 * gi], bufPsGh[2 * gi + 1])
                items = []
                for s in range(NSUB):
                    for hf in range(2):
                        items.append((gp[:, hf * 512:(hf + 1) * 512],
                                      ring[lhs][:, s * D + ic * 128: s * D + (ic + 1) * 128],
                                      ring[dz][:, s * D + hf * 512: s * D + (hf + 1) * 512],
                                      s == 0, s == NSUB - 1))
                mm_group(items, reads=(bufRing[lhs], bufRing[dz]), wpsum=bgp)
                stg, bstg = (stgA, bufStgA) if gi == 0 else (stgB, bufStgB)
                reg = bufG[(m, ic)]
                ccreg = cc_in[:, (m * NCK + ic) * D:(m * NCK + ic + 1) * D]
                if first:
                    if gi == 0:
                        scopy(stg[:, :], gp[:, :], reads=bgp, writes=(bstg,))
                    else:
                        vcopy(stg[:, :], gp[:, :], reads=bgp, writes=(bstg,))
                else:
                    dma("gpsimd", stg[:, :], ccreg, reads=(reg,), writes=(bstg,))
                    vtt(stg[:, :], gp[:, :], stg[:, :], AL.add,
                        reads=bgp + (bstg,), writes=(bstg,))
                dma("sync", ccreg, stg[:, :], reads=(bstg,), writes=(reg,))
        gp = psG[0]
        bgp = (bufPsGh[0],)
        for m, dz in enumerate(("rdz1", "rdz2")):
            for ic in range(NCK):
                col = m * NCK + ic
                items = []
                for s in range(NSUB):
                    items.append((gp[:, col:col + 1],
                                  ring[dz][:, s * D + ic * 128: s * D + (ic + 1) * 128],
                                  ones_col[:, :], s == 0, s == NSUB - 1))
                mm_group(items, reads=(bufRing[dz], bufSmall), wpsum=bgp)
        ccreg = cc_in[:, 16 * D:16 * D + 16]
        if first:
            vcopy(stgA[:, 0:16], gp[:, 0:16], reads=bgp, writes=(bufStgA,),
                  drain=True)
        else:
            dma("gpsimd", stgA[:, 0:16], ccreg, reads=(bufGb,), writes=(bufStgA,))
            vtt(stgA[:, 0:16], gp[:, 0:16], stgA[:, 0:16], AL.add,
                reads=bgp + (bufStgA,), writes=(bufStgA,), drain=True)
        dma("sync", ccreg, stgA[:, 0:16], reads=(bufStgA,), writes=(bufGb,))
        tname = "ccin%d" % st
        if tname in tap_d:
            ccdeps0 = []
            for kk in bufG.values():
                ccdeps0.extend(kk.w)
                kk.rs.append(sc.all[-1])
            ccdeps0 = []
            for kk in list(bufG.values()) + [bufGb]:
                ccdeps0.extend(kk.w)
            dma("sync", tap_d[tname][:, :], cc_in[:, :], extra=ccdeps0,
                reads=tuple(bufG.values()) + (bufGb,),
                writes=(buf("tap_" + tname),))
        grad_first[0] = False

    nstA = NST
    if variant.startswith("nst"):
        nstA = int(variant[3:4])
    for st in range(nstA):
        emit_stA(st)
        assert len(free_slots) == POOL_N, (st, free_slots, slot_of)

    ccdeps = []
    for k in bufG.values():
        ccdeps.extend(k.w)
    ccdeps.extend(bufGb.w)

    if "ccin" in tap_d:
        dma("sync", tap_d["ccin"][:, :], cc_in[:, :], extra=ccdeps,
            writes=(buf("tap_ccin"),))
    if "nocc" in variant:
        dma("gpsimd", cc_out[:, :], cc_in[:, :], extra=ccdeps, writes=(bufCCout,))
    else:
        def f_cc(e):
            return e.collective_compute(
                "AllReduce", AL.add, ins=[cc_in[:, :]], outs=[cc_out[:, :]],
                replica_groups=[[0, 1, 2, 3, 4, 5, 6, 7]])
        op_cc = sc.add("gpsimd", f_cc, extra=ccdeps, writes=(bufCCout,))
        op_cc.sem = "CC"

    def finish_early():
        if "ccread" in variant:
            dma("gpsimd", stgA[:, :], cc_out[:, 0:D], reads=(bufCCout,),
                writes=(bufStgA,))
        dma("sync", out_d[0:128, :], stgA[:, :], reads=(bufStgA,),
            writes=(buf("outearly"),))
        sc.add("gpsimd", lambda e: e.memset(ident[0:1, 0:1], 0.0), extra=buf("outearly").w)
        import contextlib
        stack = contextlib.ExitStack()
        sems = {s: stack.enter_context(nc.semaphore("sem_" + s))
                for s in ("T", "V", "S", "G", "Y", "DG", "DY", "CC")}
        sc.realize(nc, sems)
        stack.close()
        return nc

    if "cut1" in variant:
        return finish_early()

    fence5 = bufStgA.alldeps() + bufStgB.alldeps()
    prep_i = [0]

    def prep_piece(src_ap, dst_ap3, wbuf):
        i = prep_i[0] % 4
        prep_i[0] += 1
        stg, bstg = stgW[i], bufStgW[i]
        dma("gpsimd", stg[:, :], src_ap, writes=(bstg,),
            extra=fence5 + bstg.alldeps())
        ps, bps = next_ps("T")
        trans8(lambda b: stg[:, b * 128:(b + 1) * 128], ps, bps, reads=(bstg,))
        i2 = prep_i[0] % 4
        prep_i[0] += 1
        stg2, bstg2 = stgW[i2], bufStgW[i2]
        scopy(stg2[:, :], ps[:, :], reads=(bps,), writes=(bstg2,),
              extra=fence5 + bstg2.alldeps())
        dma("sync", dst_ap3, stg2.rearrange("p (b t) -> p b t", t=128),
            reads=(bstg2,), writes=(wbuf,))

    for wt_d, w_src, wbn in ((wgT_d, wg_d, "wgT"), (wuT_d, wu_d, "wuT")):
        for hc in range(NHC):
            dst = wt_d.rearrange("p (dc h) -> p dc h", h=H)[:, :, hc * 128:(hc + 1) * 128]
            prep_piece(w_src[hc * 128:(hc + 1) * 128, :], dst, bufWT_d[wbn])
    for dc in range(NCK):
        for ss in range(3):
            dst = wdT_d.rearrange("p (hc d) -> p hc d", d=D)[
                :, ss * 8:(ss + 1) * 8, dc * 128:(dc + 1) * 128]
            prep_piece(wd_d[dc * 128:(dc + 1) * 128, ss * 1024:(ss + 1) * 1024],
                       dst, bufWT_d["wdT"])

    def r5dep():
        dd = []
        for b in bufStgW:
            dd.extend(b.alldeps())
        dd.extend(bufStgP.alldeps())
        dd.extend(bufStgPb.alldeps())
        dd.extend(fence5)
        return dd

    fenceP2 = []
    for i in range(4):
        fenceP2.extend(bufPool[i].alldeps())

    for m, wn in enumerate(("W1", "W2")):
        nwt = nW1 if m == 0 else nW2
        for c in range(NCK):
            dma("gpsimd", stgP[:, :], cc_out[:, (m * NCK + c) * D:(m * NCK + c + 1) * D],
                reads=(bufCCout,), writes=(bufStgP,), extra=r5dep())
            act(stgPb[:, :], stgP[:, :], AF.Abs, reads=(bufStgP,), writes=(bufStgPb,),
                extra=r5dep())
            vts(stgPb[:, :], stgPb[:, :], EPS, None, AL.add,
                reads=(bufStgPb,), writes=(bufStgPb,))
            vrecip(stgPb[:, :], stgPb[:, :], reads=(bufStgPb,), writes=(bufStgPb,))
            vtt(stgPb[:, :], stgPb[:, :], stgP[:, :], AL.mult,
                reads=(bufStgPb, bufStgP), writes=(bufStgPb,))
            vts(stgPb[:, :], stgPb[:, :], -LR, None, AL.mult,
                reads=(bufStgPb,), writes=(bufStgPb,))
            vstt(nwt[:, c * D:(c + 1) * D], Wsb[wn][:, c * D:(c + 1) * D],
                 1.0 - LR * WD, stgPb[:, :], AL.mult, AL.add,
                 reads=(bufW[wn], bufStgPb),
                 writes=(bnW[wn],) if c == NCK - 1 else (),
                 extra=fenceP2 if c == 0 else ())
    for m in range(2):
        nbn = "nb%d" % (m + 1)
        bn = "b%d" % (m + 1)
        bga, bgb = buf("gA"), buf("gB")
        dma("gpsimd", gA[:, :], cc_out[:, 16 * D + 8 * m:16 * D + 8 * m + 8],
            reads=(bufCCout,), writes=(bga,))
        act(gB[:, :], gA[:, :], AF.Abs, reads=(bga,), writes=(bgb,), drain=True)
        vts(gB[:, :], gB[:, :], EPS, None, AL.add, reads=(bgb,), writes=(bgb,),
            drain=True)
        vrecip(gB[:, :], gB[:, :], reads=(bgb,), writes=(bgb,), drain=True)
        vtt(gB[:, :], gB[:, :], gA[:, :], AL.mult, reads=(bgb, bga), writes=(bgb,),
            drain=True)
        vts(gB[:, :], gB[:, :], -LR, None, AL.mult, reads=(bgb,), writes=(bgb,),
            drain=True)
        def fstt(e, nbn=nbn, bn=bn):
            return e.scalar_tensor_tensor(out=bias_sb[nbn][:, :],
                                          in0=bias_sb[bn][:, :],
                                          scalar=1.0 - LR * WD, in1=gB[:, :],
                                          op0=AL.mult, op1=AL.add)
        sc.add("vector", fstt, reads=(bufBias[bn], bgb),
               writes=(bufBias[nbn],), drain=True)

    if "grads" in tap_d:
        dma("sync", tap_d["grads"][:, :], cc_out[:, :], reads=(bufCCout,),
            writes=(buf("tap_g"),))
    if "nW1" in tap_d:
        dma("sync", tap_d["nW1"][:, :], nW1[:, :], reads=(bnW["W1"],),
            writes=(buf("tap_nW1"),))

    if "cut2" in variant:
        return finish_early()

    fence3 = []
    for b in bufPool:
        fence3.extend(b.alldeps())

    def emit_stB(st):
        f3 = fence3 if st == 0 else ()
        dma("sync", bQ[:, :], q_d[st * 128:(st + 1) * 128, :],
            reads=(bufQd[st],), writes=(bufQ,), extra=f3)
        for oc in range(NCK):
            ps, bps = next_ps("M")
            mm_group([(ps[:, :], nW1[:, kc * D + oc * 128: kc * D + (oc + 1) * 128],
                       bQ[:, kc * ST:(kc + 1) * ST], kc == 0, kc == NCK - 1)
                      for kc in range(NCK)],
                     reads=(bnW["W1"], bufQ), wpsum=bps, extra=f3)
            act(bTA[:, oc * ST:(oc + 1) * ST], ps[:, :], AF.Silu,
                bias=bias_sb["nb1"][:, oc:oc + 1], reads=(bps, bufBias["nb1"]),
                writes=(bufTA,) if oc == NCK - 1 else (),
                extra=(list(f3) + bufTA.alldeps()) if oc == 0 else ())
        vtt(bQH1[:, :], bQ[:, :], bTA[:, :], AL.add, reads=(bufQ, bufTA),
            writes=(bufQH1,), extra=f3)
        for oc in range(NCK):
            ps, bps = next_ps("M")
            mm_group([(ps[:, :], nW2[:, kc * D + oc * 128: kc * D + (oc + 1) * 128],
                       bQH1[:, kc * ST:(kc + 1) * ST], kc == 0, kc == NCK - 1)
                      for kc in range(NCK)],
                     reads=(bnW["W2"], bufQH1), wpsum=bps)
            act(bTA[:, oc * ST:(oc + 1) * ST], ps[:, :], AF.Silu,
                bias=bias_sb["nb2"][:, oc:oc + 1], reads=(bps, bufBias["nb2"]),
                writes=(bufTA,) if oc == NCK - 1 else (),
                extra=bufTA.alldeps() if oc == 0 else ())
        vtt(bQ[:, :], bQH1[:, :], bTA[:, :], AL.add, reads=(bufQH1, bufTA),
            writes=(bufQ,))
        if "retr" in tap_d and st == 0:
            dma("sync", tap_d["retr"][:, :], bQ[:, :], reads=(bufQ,),
                writes=(buf("tap_r"),))
        for s in range(NSUB):
            ps, bps = next_ps("T")
            trans8(lambda c, s=s: bQ[:, c * ST + s * 128: c * ST + (s + 1) * 128],
                   ps, bps, reads=(bufQ,))
            act(bRT[:, s * D:(s + 1) * D], ps[:, :], AF.Identity,
                accum=stat[:, s:s + 1], reads=(bps,),
                writes=((bufRT,) if s == NSUB - 1 else ()) + (bufStat,),
                extra=(list(f3) + bufRT.alldeps()
                       + (bufStgA.alldeps() + bufStgB.alldeps()
                          + bufStgP.alldeps() + bufStgPb.alldeps()
                          + [d for b_ in bufStgW for d in b_.alldeps()])
                       if st == 0 else []) if s == 0 else ())
            act(bTA[:, s * D:(s + 1) * D], bRT[:, s * D:(s + 1) * D], AF.Square,
                accum=stat[:, 4 + s:5 + s], reads=(bufRT,), writes=(bufTA, bufStat))
        ln_minimath()
        for s in range(NSUB):
            dma("gpsimd", bQH1[:, s * D:(s + 1) * D],
                x_d[st * ST + s * 128: st * ST + (s + 1) * 128, :],
                writes=(bufQH1,) if s == NSUB - 1 else (),
                extra=(bufQH1.alldeps() + list(f3)) if s == 0 else ())
        for s in range(NSUB):
            vts(bQH1[:, s * D:(s + 1) * D], bQH1[:, s * D:(s + 1) * D],
                stat[:, 448 + s:449 + s], None, AL.add,
                reads=(bufQH1, bufStat), writes=(bufQH1,))
            vstt(bTA[:, s * D:(s + 1) * D], bRT[:, s * D:(s + 1) * D],
                 stat[:, 384 + s:385 + s], bQH1[:, s * D:(s + 1) * D], AL.mult, AL.add,
                 reads=(bufRT, bufQH1, bufStat),
                 writes=(bufTA,) if s == NSUB - 1 else (),
                 extra=bufTA.alldeps() if s == 0 else ())
        dma("sync", xp_d[st * ST:(st + 1) * ST, :].rearrange("(s p) d -> p s d", p=128),
            bTA.rearrange("p (s d) -> p s d", d=D), reads=(bufTA,), writes=(bufXpd[st],))
        if "xp" in tap_d and st == 0:
            dma("sync", tap_d["xp"][:, :], bTA[:, :], reads=(bufTA,),
                writes=(buf("tap_xp"),))

    for st in range(NST):
        emit_stB(st)

    if "cut3" in variant:
        return finish_early()

    fR1 = []
    for n in WNAMES:
        fR1.extend(bufW[n].alldeps())
    fR2 = []
    for b in bufRing.values():
        fR2.extend(b.alldeps())
    fR2.extend(r5dep())
    dma("sync", wgT[:, :], wgT_d[:, :], reads=(bufWT_d["wgT"],),
        writes=(bufFW["wgT"],), extra=fR1)
    dma("sync", wuT[:, :], wuT_d[:, :], reads=(bufWT_d["wuT"],),
        writes=(bufFW["wuT"],), extra=fR1)
    dma("sync", wdTa[:, :], wdT_d[:, 0:16 * D], reads=(bufWT_d["wdT"],),
        writes=(bufFW["wdTa"],), extra=fR2)
    fRW = (bufStgA.alldeps() + bufStgB.alldeps() + bufStgP.alldeps()
           + bufStgPb.alldeps() + bufRT.alldeps())
    for b_ in bufStgW:
        fRW.extend(b_.alldeps())
    dma("sync", wdTb[:, :], wdT_d[:, 16 * D:24 * D], reads=(bufWT_d["wdT"],),
        writes=(bufFW["wdTb"],), extra=fRW)

    fenceB = []
    for b in (bufQ, bufTA, bufQH1, bufRT, bnW["W1"], bnW["W2"]):
        fenceB.extend(b.alldeps())

    pool_mm[0] = [(psM[0], bufPsM[0]), (psM[1], bufPsM[1])]

    def emit_stF(st):
        fB = fenceB if st == 0 else ()
        dma("sync", fXP.rearrange("p (s d) -> p s d", d=D),
            xp_d[st * ST:(st + 1) * ST, :].rearrange("(s p) d -> p s d", p=128),
            reads=(bufXpd[st],), writes=(bufFXP,), extra=list(fB) + bufFXP.alldeps())
        ln_stats(fXP, bufFXP, fLN_tm, bufFLNtm,
                 extra0=list(fB) + bufFLNtm.alldeps())
        for s in range(NSUB):
            vts(fLN_tm[:, s * D:(s + 1) * D], fXP[:, s * D:(s + 1) * D],
                stat[:, 384 + s:385 + s], stat[:, 448 + s:449 + s], AL.mult, AL.add,
                reads=(bufFXP, bufStat), writes=(bufFLNtm,),
                extra=list(fB) if s == 0 else ())
            ps, bps = next_ps("T")
            trans8(lambda c, s=s: fLN_tm[:, s * D + c * 128: s * D + (c + 1) * 128],
                   ps, bps, reads=(bufFLNtm,))
            scopy(fLN_fm.rearrange("p (c t) -> p c t", t=ST)[:, :, s * 128:(s + 1) * 128],
                  ps.rearrange("p (c t) -> p c t", t=128), reads=(bps,),
                  writes=(bufFLNfm,) if s == NSUB - 1 else (),
                  extra=(list(fB) + bufFLNfm.alldeps()) if s == 0 else ())
        ffin_dead = bufFFIN.alldeps()
        for hc in range(NHC):
            psg, bpsg = next_ps("M")
            mm_group([(psg[:, :], wgT[:, kc * H + hc * 128: kc * H + (hc + 1) * 128],
                       fLN_fm[:, kc * ST:(kc + 1) * ST], kc == 0, kc == NCK - 1)
                      for kc in range(NCK)],
                     reads=(bufFW["wgT"], bufFLNfm), wpsum=bpsg)
            psu, bpsu = next_ps("M")
            mm_group([(psu[:, :], wuT[:, kc * H + hc * 128: kc * H + (hc + 1) * 128],
                       fLN_fm[:, kc * ST:(kc + 1) * ST], kc == 0, kc == NCK - 1)
                      for kc in range(NCK)],
                     reads=(bufFW["wuT"], bufFLNfm), wpsum=bpsu)
            act(fH[:, hc * ST:(hc + 1) * ST], psg[:, :], AF.Silu, scale=1.0 / 127.0,
                reads=(bpsg,), writes=(bufFH,),
                extra=ffin_dead if hc == 0 else ())

            def fmul(e, hc=hc, psu=psu):
                return e.tensor_tensor(out=fH[:, hc * ST:(hc + 1) * ST],
                                       in0=fH[:, hc * ST:(hc + 1) * ST],
                                       in1=psu[:, :], op=AL.mult)
            sc.add("vector", fmul, reads=(bufFH, bpsu), writes=(bufFH,))
        for s in range(NSUB):
            gi = s % 2
            gp = psG[gi]
            bgp = (bufPsGh[2 * gi], bufPsGh[2 * gi + 1])
            items = []
            for hc in range(NHC):
                for hf in range(2):
                    items.append((gp[:, hf * 512:(hf + 1) * 512],
                                  fH[:, hc * ST + s * 128: hc * ST + (s + 1) * 128],
                                  wd_ap(hc, hf),
                                  hc == 0, hc == NHC - 1))
            mm_group(items, reads=(bufFH, bufFW["wdTa"], bufFW["wdTb"]), wpsum=bgp)
            act(fOUT[:, s * D:(s + 1) * D], gp[:, :], AF.Identity,
                accum=stat[:, s:s + 1], reads=bgp, writes=(bufFOUT, bufStat))
            act(fLN_tm[:, s * D:(s + 1) * D], fOUT[:, s * D:(s + 1) * D], AF.Square,
                accum=stat[:, 4 + s:5 + s], reads=(bufFOUT,),
                writes=(bufFLNtm, bufStat))
        ln_minimath()
        fh_dead = bufFH.alldeps()
        for s in range(NSUB):
            vstt(fFIN[:, s * D:(s + 1) * D], fOUT[:, s * D:(s + 1) * D],
                 stat[:, 384 + s:385 + s], fXP[:, s * D:(s + 1) * D], AL.mult, AL.add,
                 reads=(bufFOUT, bufFXP, bufStat), writes=(bufFFIN,),
                 extra=fh_dead if s == 0 else ())
            vts(fFIN[:, s * D:(s + 1) * D], fFIN[:, s * D:(s + 1) * D],
                stat[:, 448 + s:449 + s], None, AL.add,
                reads=(bufFFIN, bufStat), writes=(bufFFIN,))
            dma("sync", out_d[st * ST + s * 128: st * ST + (s + 1) * 128, :],
                fFIN[:, s * D:(s + 1) * D], reads=(bufFFIN,),
                writes=(buf("out%d_%d" % (st, s)),))

    for st in range(NST):
        emit_stF(st)

    outdeps = []
    for st in range(NST):
        for s in range(NSUB):
            outdeps.extend(buf("out%d_%d" % (st, s)).w)
    sc.add("gpsimd", lambda e: e.memset(ident[0:1, 0:1], 0.0), extra=outdeps)

    import contextlib
    stack = contextlib.ExitStack()
    sems = {s: stack.enter_context(nc.semaphore("sem_" + s)) for s in SEMNAMES}
    sc.realize(nc, sems)
    stack.close()
    return nc


_NC_CACHE = {}


def get_nc(taps=()):
    key = tuple(taps)
    if key not in _NC_CACHE:
        _NC_CACHE[key] = build_nc(taps)
    return _NC_CACHE[key]


def kernel(**inputs):
    nc = get_nc()
    full_x = np.ascontiguousarray(np.asarray(inputs["L0_attn_x"], dtype=np.float32))
    assert full_x.shape == (8, 4096, 1024)
    base = {}
    for n in ("L0_attn_Wk", "L0_attn_bk", "L0_attn_Wq", "L0_attn_bq", "L0_attn_Wv",
              "L0_attn_bv", "L0_attn_Wlr", "L0_attn_blr", "L0_attn_W1", "L0_attn_b1",
              "L0_attn_W2", "L0_attn_b2"):
        base[n] = np.ascontiguousarray(np.asarray(inputs[n], dtype=np.float32))
    for n in ("L0_ffn_w_gate", "L0_ffn_w_up", "L0_ffn_w_down"):
        base[n] = np.ascontiguousarray(np.asarray(inputs[n], dtype=np.int32))
    in_maps = [dict(base, L0_attn_x=full_x[c]) for c in range(8)]
    res = run_bass_kernel_spmd(nc, in_maps, core_ids=list(range(8)))
    out = np.stack([res.results[c]["out"] for c in range(8)], axis=0)
    return np.ascontiguousarray(out.astype(np.float32))


# version 09
# speedup vs baseline: 1.0481x; 1.0481x over previous
import numpy as np
from collections import defaultdict

import concourse.bass as bass
import concourse.mybir as mybir
from concourse.bass_utils import run_bass_kernel_spmd

F32 = mybir.dt.float32
BF16 = mybir.dt.bfloat16
I32 = mybir.dt.int32
AF = mybir.ActivationFunctionType
AL = mybir.AluOpType

D = 1024
NCK = 8
TOK = 4096
ST = 512
NST = TOK // ST
NSUB = ST // 128
H = 3072
NHC = H // 128
LR = 0.01
WD = 0.01
EPS = 1e-08
LN_EPS = 1e-5
ALRF = -(2.0 / D) * 0.1

ENGS = ("tensor", "vector", "scalar", "gpsimd", "sync")
ESEM = {"tensor": "T", "vector": "V", "scalar": "S", "gpsimd": "G", "sync": "Y"}
NQROT = 6
QSEM = {"gpsimd": "DG", "sync": "DY"}
SEMNAMES = (["T", "V", "S", "G", "Y", "CC"]
            + [f"DG{i}" for i in range(NQROT)] + [f"DY{i}" for i in range(NQROT)])


class Op:
    __slots__ = ("eng", "fn", "deps", "sem", "inc", "marked", "val", "drain")

    def __init__(self, eng, fn, deps, sem, inc):
        self.eng, self.fn, self.deps, self.sem, self.inc = eng, fn, deps, sem, inc
        self.marked = False
        self.val = None
        self.drain = False


class Buf:
    __slots__ = ("name", "w", "rs")

    def __init__(self, name):
        self.name = name
        self.w = []
        self.rs = []

    def alldeps(self):
        return list(self.w) + list(self.rs)


class Sched:
    def __init__(self):
        self.ops = {e: [] for e in ENGS}
        self.all = []
        self.qrot = {"gpsimd": 0, "sync": 0}

    def add(self, eng, fn, reads=(), writes=(), extra=(), dma=False, drain=False):
        deps = []
        for b in reads:
            deps.extend(b.w)
        for b in writes:
            deps.extend(b.w)
            deps.extend(b.rs)
        deps.extend(extra)
        if dma:
            sem = QSEM[eng] + str(self.qrot[eng] % NQROT)
            self.qrot[eng] += 1
        else:
            sem = ESEM[eng]
        op = Op(eng, fn, deps, sem, 16 if dma else 1)
        op.drain = drain
        self.ops[eng].append(op)
        self.all.append(op)
        for b in reads:
            b.rs.append(op)
        for b in writes:
            b.w = [op]
            b.rs = []
        if dma:
            op.marked = True
        return op

    def realize(self, nc, sems):
        for op in self.all:
            if op.drain:
                op.marked = True
            for d in op.deps:
                if not (d.eng == op.eng and d.sem == ESEM[op.eng]):
                    d.marked = True
        cnt = defaultdict(int)
        for e in ENGS:
            for op in self.ops[e]:
                if op.marked:
                    cnt[op.sem] += op.inc
                    op.val = cnt[op.sem]

        with nc.Block() as block:
            def runner(eng):
                def go(e):
                    floor = defaultdict(int)
                    for op in self.ops[eng]:
                        want = {}
                        for d in op.deps:
                            if d.eng == eng and d.sem == ESEM[eng]:
                                continue
                            if want.get(d.sem, 0) < d.val:
                                want[d.sem] = d.val
                        for s, v in sorted(want.items()):
                            if v > floor[s]:
                                e.wait_ge(sems[s], v)
                                floor[s] = v
                        ins = op.fn(e)
                        if op.marked:
                            ins.then_inc(sems[op.sem], op.inc)
                        if op.drain:
                            e.wait_ge(sems[op.sem], op.val)
                            floor[op.sem] = max(floor[op.sem], op.val)
                return go
            for eng in ENGS:
                getattr(block, eng)(runner(eng))


def build_nc(taps=(), variant=""):
    nc = bass.Bass("TRN2", target_bir_lowering=False, debug=False)
    sc = Sched()

    x_d = nc.dram_tensor("L0_attn_x", [TOK, D], F32, kind="ExternalInput").ap()
    W_d = {}
    for n in ("Wk", "Wq", "Wv", "W1", "W2"):
        W_d[n] = nc.dram_tensor("L0_attn_" + n, [D, D], F32, kind="ExternalInput").ap()
    b_d = {}
    for n in ("bk", "bq", "bv", "b1", "b2"):
        b_d[n] = nc.dram_tensor("L0_attn_" + n, [D], F32, kind="ExternalInput").ap()
    Wlr_d = nc.dram_tensor("L0_attn_Wlr", [D, 1], F32, kind="ExternalInput").ap()
    blr_d = nc.dram_tensor("L0_attn_blr", [1], F32, kind="ExternalInput").ap()
    wg_d = nc.dram_tensor("L0_ffn_w_gate", [H, D], I32, kind="ExternalInput").ap()
    wu_d = nc.dram_tensor("L0_ffn_w_up", [H, D], I32, kind="ExternalInput").ap()
    wd_d = nc.dram_tensor("L0_ffn_w_down", [D, H], I32, kind="ExternalInput").ap()
    out_d = nc.dram_tensor("out", [TOK, D], F32, kind="ExternalOutput").ap()

    if "noint" in variant:
        q_d = xp_d = None
    else:
        q_d = nc.dram_tensor("q_spill", [NST * 128, NCK * ST], BF16, kind="Internal").ap()
        xp_d = nc.dram_tensor("xp_spill", [TOK, D], BF16, kind="Internal").ap()
    CCW = 8 * D + 8
    cc_in = nc.dram_tensor("cc_in", [128, CCW], F32, kind="Internal").ap()
    cc_out = nc.dram_tensor("cc_out", [128, CCW], F32, kind="Internal",
                            addr_space="Shared").ap()
    cc2_in = nc.dram_tensor("cc2_in", [128, CCW], F32, kind="Internal").ap()
    cc2_out = nc.dram_tensor("cc2_out", [128, CCW], F32, kind="Internal",
                             addr_space="Shared").ap()
    if "noint" in variant:
        wgT_d = wuT_d = wdT_d = None
    else:
        wgT_d = nc.dram_tensor("wgT_d", [128, NCK * H], BF16, kind="Internal").ap()
        wuT_d = nc.dram_tensor("wuT_d", [128, NCK * H], BF16, kind="Internal").ap()
        wdT_d = nc.dram_tensor("wdT_d", [128, NHC * D], BF16, kind="Internal").ap()

    tap_d = {}
    for tname, shape, dt in taps:
        tap_d[tname] = nc.dram_tensor("tap_" + tname, shape, dt,
                                      kind="ExternalOutput").ap()

    OFF = {}
    pos = 0

    def place(name, nbytes):
        nonlocal pos
        OFF[name] = pos
        pos += (nbytes + 31) & ~31

    place("R1", 98304)
    place("R2", 32768)
    place("RW", 16384)
    place("R3", 57344)
    place("SMALL", 4096)
    arena = nc.alloc_sbuf_tensor("arena", [128, pos], mybir.dt.uint8)
    base = nc.lookup_mloc(arena).addr

    _atcnt = [0]

    def sb(name, shape, dtype, region, roff):
        if "noat" in variant:
            _atcnt[0] += 1
            if _atcnt[0] > 1:
                return None
            return nc.alloc_sbuf_tensor_at(name, shape, dtype,
                                           offset=base + OFF["R5"]).ap()
        return nc.alloc_sbuf_tensor_at(name, shape, dtype,
                                       offset=base + OFF[region] + roff).ap()

    soff = [0]

    def small(name, shape, dtype):
        t = sb(name, shape, dtype, "SMALL", soff[0])
        soff[0] += (int(np.prod(shape[1:])) * mybir.dt.size(dtype) + 31) & ~31
        return t

    ident = small("ident", [128, 128], BF16)
    ones_col = small("ones_col", [128, 1], BF16)
    ones_row = small("ones_row", [1, 128], F32)
    blr_bc = small("blr_bc", [128, 1], F32)
    bias_sb = {n: small("c_" + n, [128, NCK], F32)
               for n in ("bk", "bq", "bv", "b1", "b2", "nb1", "nb2")}
    wlr_sb = small("wlr", [128, NCK], BF16)
    stat = small("stat", [128, 512], F32)
    alrS = small("alrS", [128, 64], F32)
    svR = small("svR", [128, 32], F32)
    svMR = small("svMR", [128, 32], F32)
    gA = small("gA", [128, NCK], F32)
    gB = small("gB", [128, NCK], F32)
    alrneg = small("alrneg", [128, NSUB], F32)
    assert soff[0] <= 4096, soff[0]

    WNAMES = ("Wk", "Wq", "Wv", "W1", "W2", "W2T")
    Wsb = {n: sb(n, [128, NCK * D], BF16, "R1", i * 16384)
           for i, n in enumerate(WNAMES)}
    ring = {n: sb(n, [128, NSUB * D], BF16, "R2", i * 8192)
            for i, n in enumerate(("rk", "rh1", "rdz1", "rdz2"))}
    stgA = sb("stgA", [128, D], F32, "RW", 0)
    stgB = sb("stgB", [128, D], F32, "RW", 4096)
    POOL_N = 7
    pool = [sb(f"pool{i}", [128, NCK * ST], BF16, "R3", i * 8192)
            for i in range(POOL_N)]
    nW1 = sb("nW1", [128, NCK * D], BF16, "R3", 0)
    nW2 = sb("nW2", [128, NCK * D], BF16, "R3", 16384)
    bQ = sb("bQ", [128, NCK * ST], BF16, "R3", 32768)
    bTA = sb("bTA", [128, NCK * ST], BF16, "R3", 40960)
    bQH1 = sb("bQH1", [128, NCK * ST], BF16, "R3", 49152)
    bRT = sb("bRT", [128, NSUB * D], BF16, "RW", 0)
    stgP = sb("stgP", [128, D], F32, "RW", 0)
    stgPb = sb("stgPb", [128, D], F32, "RW", 4096)
    stgW = [sb(f"stgW{i}", [128, D], BF16, "RW", 8192 + i * 2048) for i in range(4)]
    wgT = sb("wgT", [128, NCK * H], BF16, "R1", 0)
    wuT = sb("wuT", [128, NCK * H], BF16, "R1", 49152)
    wdTa = sb("wdTa", [128, 16 * D], BF16, "R2", 0)
    wdTb = sb("wdTb", [128, 8 * D], BF16, "RW", 0)
    fXP = sb("fXP", [128, NSUB * D], BF16, "R3", 0)
    fLN_tm = sb("fLN_tm", [128, NSUB * D], BF16, "R3", 8192)
    fLN_fm = sb("fLN_fm", [128, NCK * ST], BF16, "R3", 16384)
    fOUT = sb("fOUT", [128, NSUB * D], BF16, "R3", 24576)
    fH = sb("fH", [128, NHC * ST], BF16, "R3", 32768)
    fFIN = sb("fFIN", [128, NSUB * D], F32, "R3", 32768)

    if "nopsum" in variant:
        psM = psT = psG = [None, None]
    else:
        psM = [nc.alloc_psum_tensor("psM0", [128, 512], F32).ap(),
               nc.alloc_psum_tensor("psM1", [128, 512], F32).ap()]
        psT = [nc.alloc_psum_tensor("psT0", [128, 1024], BF16).ap(),
               nc.alloc_psum_tensor("psT1", [128, 1024], BF16).ap()]
        psG = [nc.alloc_psum_tensor("psG0", [128, 1024], F32).ap(),
               nc.alloc_psum_tensor("psG1", [128, 1024], F32).ap()]

    _B = {}

    def buf(name):
        if name not in _B:
            _B[name] = Buf(name)
        return _B[name]

    bufW = {n: buf("W_" + n) for n in WNAMES}
    bufRing = {n: buf(n) for n in ring}
    bufPool = [buf(f"pool{i}") for i in range(POOL_N)]
    bufStgA, bufStgB = buf("stgA"), buf("stgB")
    bufPsM = [buf("psM0"), buf("psM1")]
    bufPsT = [buf("psT0"), buf("psT1")]
    bufPsGh = [buf("psG0a"), buf("psG0b"), buf("psG1a"), buf("psG1b")]
    bufStat = buf("stat")
    bufAlr = buf("alrneg")
    bufSmall = buf("small")
    bufBias = {n: buf("bias_" + n) for n in bias_sb}
    bufQd = [buf(f"q_d{st}") for st in range(NST)]
    bufXpd = [buf(f"xp_d{st}") for st in range(NST)]
    bufG = {(m, c): buf(f"g{m}_{c}") for m in range(2) for c in range(NCK)}
    bufGb = buf("gbias")
    bufCCout = buf("cc_out")
    bufWT_d = {n: buf(n + "_dram") for n in ("wgT", "wuT", "wdT")}
    bufFW = {n: buf("sb_" + n) for n in ("wgT", "wuT", "wdTa", "wdTb")}

    def wd_ap(hc, hf):
        if hc < 16:
            return wdTa[:, hc * D + hf * 512: hc * D + hf * 512 + 512]
        return wdTb[:, (hc - 16) * D + hf * 512: (hc - 16) * D + hf * 512 + 512]
    bufStgP, bufStgPb = buf("stgP"), buf("stgPb")
    bufStgW = [buf(f"stgW{i}") for i in range(4)]
    bnW = {"W1": buf("nW1"), "W2": buf("nW2")}
    bufQ, bufTA, bufQH1 = buf("bQ"), buf("bTA"), buf("bQH1")
    bufRT = buf("bRT")
    bufFXP, bufFLNtm, bufFLNfm = buf("fXP"), buf("fLN_tm"), buf("fLN_fm")
    bufFOUT, bufFH, bufFFIN = buf("fOUT"), buf("fH"), buf("fFIN")

    rot = {"M": [0], "T": [0]}
    pool_full = [(psM[0], bufPsM[0]), (psM[1], bufPsM[1]),
                 (psG[0][:, 0:512], bufPsGh[0]), (psG[0][:, 512:1024], bufPsGh[1]),
                 (psG[1][:, 0:512], bufPsGh[2]), (psG[1][:, 512:1024], bufPsGh[3])]
    pool_mm = [pool_full]

    def next_ps(kind):
        if kind == "T":
            i = rot["T"][0] % 2
            rot["T"][0] += 1
            return psT[i], bufPsT[i]
        p = pool_mm[0]
        i = rot["M"][0] % len(p)
        rot["M"][0] += 1
        return p[i]

    def mm_group(items, reads, wpsum, extra=()):
        wps = wpsum if isinstance(wpsum, tuple) else (wpsum,)
        n = len(items)
        war = list(extra)
        for w in wps:
            war.extend(w.alldeps())
        for j, (o, l, r, st_, sp_) in enumerate(items):
            def f(e, o=o, l=l, r=r, st_=st_, sp_=sp_):
                return e.matmul(o, l, r, start=st_, stop=sp_)
            sc.add("tensor", f,
                   reads=reads if (j == 0 or j == n - 1) else (),
                   writes=wps if j == n - 1 else (),
                   extra=war if j == 0 else ())

    def trans8(src_fn, ps, bps, reads):
        for c in range(NCK):
            def f(e, c=c):
                return e.transpose(ps[:, c * 128:(c + 1) * 128], src_fn(c), ident[:, :])
            sc.add("tensor", f,
                   reads=reads if (c == 0 or c == NCK - 1) else (),
                   writes=(bps,) if c == NCK - 1 else (),
                   extra=(bps.alldeps() + list(bufSmall.w)) if c == 0 else ())

    def act(out, in_, func, bias=0.0, scale=1.0, accum=None,
            reads=(), writes=(), extra=(), drain=False):
        def f(e):
            kw = {"accum_out": accum} if accum is not None else {}
            return e.activation(out, in_, func, bias=bias, scale=scale, **kw)
        sc.add("scalar", f, reads=reads, writes=writes, extra=extra, drain=drain)

    def vts(out, in0, s1, s2, op0, op1=None, reads=(), writes=(), extra=(),
            drain=False):
        def f(e):
            if s2 is None:
                return e.tensor_scalar(out=out, in0=in0, scalar1=s1, scalar2=None,
                                       op0=op0)
            return e.tensor_scalar(out=out, in0=in0, scalar1=s1, scalar2=s2,
                                   op0=op0, op1=op1)
        sc.add("vector", f, reads=reads, writes=writes, extra=extra, drain=drain)

    def vtt(out, a, b_, op, reads=(), writes=(), extra=(), drain=False):
        def f(e):
            return e.tensor_tensor(out=out, in0=a, in1=b_, op=op)
        sc.add("vector", f, reads=reads, writes=writes, extra=extra, drain=drain)

    def vstt(out, in0, scalar, in1, op0, op1, reads=(), writes=(), extra=()):
        def f(e):
            return e.scalar_tensor_tensor(out=out, in0=in0, scalar=scalar, in1=in1,
                                          op0=op0, op1=op1)
        sc.add("vector", f, reads=reads, writes=writes, extra=extra)

    def vcopy(out, in_, reads=(), writes=(), extra=(), drain=False):
        sc.add("vector", lambda e: e.tensor_copy(out, in_),
               reads=reads, writes=writes, extra=extra, drain=drain)

    def scopy(out, in_, reads=(), writes=(), extra=(), drain=False):
        sc.add("scalar", lambda e: e.copy(out, in_),
               reads=reads, writes=writes, extra=extra, drain=drain)

    def vrecip(out, in_, reads=(), writes=(), drain=False):
        sc.add("vector", lambda e: e.reciprocal(out, in_), reads=reads,
               writes=writes, drain=drain)

    def dma(q, out, in_, reads=(), writes=(), extra=(), **kw):
        return sc.add(q, lambda e: e.dma_start(out, in_, **kw),
                      reads=reads, writes=writes, extra=extra, dma=True)

    def ln_stats(src, bsrc, scr, bscr, extra0=()):
        for s in range(NSUB):
            act(scr[:, s * D:(s + 1) * D], src[:, s * D:(s + 1) * D], AF.Identity,
                accum=stat[:, s:s + 1], reads=(bsrc,), writes=(bscr, bufStat),
                extra=extra0 if s == 0 else ())
            act(scr[:, s * D:(s + 1) * D], src[:, s * D:(s + 1) * D], AF.Square,
                accum=stat[:, 4 + s:5 + s], reads=(bsrc,), writes=(bscr, bufStat))
        ln_minimath()

    def ln_minimath():
        vts(stat[:, 64:128], stat[:, 0:64], -1.0 / D, None, AL.mult,
            reads=(bufStat,), writes=(bufStat,))
        vtt(stat[:, 128:192], stat[:, 64:128], stat[:, 64:128], AL.mult,
            reads=(bufStat,), writes=(bufStat,))
        vtt(stat[:, 192:256], stat[:, 68:132], stat[:, 128:192], AL.add,
            reads=(bufStat,), writes=(bufStat,))
        vts(stat[:, 256:320], stat[:, 192:256], -1.0, LN_EPS, AL.mult, AL.add,
            reads=(bufStat,), writes=(bufStat,))
        act(stat[:, 320:384], stat[:, 256:320], AF.Sqrt, reads=(bufStat,),
            writes=(bufStat,))
        vrecip(stat[:, 384:448], stat[:, 320:384], reads=(bufStat,),
               writes=(bufStat,))
        vtt(stat[:, 448:512], stat[:, 64:128], stat[:, 384:448], AL.mult,
            reads=(bufStat,), writes=(bufStat,))

    skip = lambda tag: ("q" + tag) in variant
    if not skip("1"):
        sc.add("gpsimd", lambda e: e.memset(ident[:, :], 0.0), writes=(bufSmall,))

    def f_id(e):
        return e.affine_select(out=ident[:, :], in_=ident[:, :],
                               compare_op=AL.not_equal, fill=1.0, base=0,
                               pattern=[[-1, 128]], channel_multiplier=1)
    if not skip("1"):
        sc.add("gpsimd", f_id, reads=(bufSmall,), writes=(bufSmall,))
        sc.add("gpsimd", lambda e: e.memset(ones_col[:, :], 1.0), writes=(bufSmall,))
    if not skip("2"):
        for n in ("bk", "bq", "bv", "b1", "b2"):
            dma("sync", bias_sb[n][:, :], b_d[n].rearrange("(c p) -> p c", p=128),
                writes=(bufBias[n],), allow_slow_non_contiguous=True)
        dma("gpsimd", wlr_sb[:, :], Wlr_d.rearrange("(c p) o -> p (c o)", p=128),
            writes=(bufSmall,), allow_slow_non_contiguous=True)
        dma("sync", blr_bc[:1, :], blr_d[None, :], writes=(bufSmall,))
        sc.add("gpsimd", lambda e: e.memset(ones_row[:, :], 1.0), writes=(bufSmall,))
        _psb, _bpsb = next_ps("M")
        sc.add("tensor",
               lambda e: e.matmul(_psb[:, 0:1], ones_row[:, :], blr_bc[:1, 0:1],
                                  start=True, stop=True),
               reads=(bufSmall,), writes=(_bpsb,), extra=_bpsb.alldeps())
        scopy(blr_bc[:, :], _psb[:, 0:1], reads=(_bpsb,), writes=(bufSmall,),
              drain=True)
    if not skip("3"):
        for n in ("Wk", "Wq", "Wv", "W1", "W2"):
            for kc in range(NCK):
                dma("gpsimd", Wsb[n][:, kc * D:(kc + 1) * D],
                    W_d[n][kc * 128:(kc + 1) * 128, :],
                    writes=(bufW[n],) if kc == NCK - 1 else (),
                    extra=bufW[n].alldeps() if kc == 0 else ())
    if not skip("4"):
        for kc in range(NCK):
            ps, bps = next_ps("T")
            trans8(lambda ic, kc=kc: Wsb["W2"][:, ic * D + kc * 128: ic * D + kc * 128 + 128],
                   ps, bps, reads=(bufW["W2"],))
            scopy(Wsb["W2T"][:, kc * D:(kc + 1) * D], ps[:, :], reads=(bps,),
                  writes=(bufW["W2T"],) if kc == NCK - 1 else (),
                  extra=bufW["W2T"].alldeps() if kc == 0 else ())

    bSvR, bSvMR = buf("svR"), buf("svMR")
    pre_pool = [(sb("preX0", [128, NSUB * D], BF16, "R3", 0), buf("preX0")),
                (sb("preX1", [128, NSUB * D], BF16, "R3", 8192), buf("preX1")),
                (sb("preS0", [128, NSUB * D], BF16, "R3", 16384), buf("preS0")),
                (sb("preS1", [128, NSUB * D], BF16, "R3", 24576), buf("preS1"))]

    def emit_prepass(st):
        X, bX_ = pre_pool[st % 2]
        SCR, bSCR = pre_pool[2 + st % 2]
        for s in range(NSUB):
            dma("gpsimd", X[:, s * D:(s + 1) * D],
                x_d[st * ST + s * 128: st * ST + (s + 1) * 128, :],
                writes=(bX_,) if s == NSUB - 1 else (),
                extra=bX_.alldeps() if s == 0 else ())
        ln_stats(X, bX_, SCR, bSCR)
        vcopy(svR[:, st * 4:st * 4 + 4], stat[:, 384:388], reads=(bufStat,),
              writes=(bSvR,), drain=True)
        vcopy(svMR[:, st * 4:st * 4 + 4], stat[:, 448:452], reads=(bufStat,),
              writes=(bSvMR,), drain=True)

    for st in range(NST):
        emit_prepass(st)
    prefence = []
    for _, b_ in pre_pool:
        prefence.extend(b_.alldeps())
    for i in range(4):
        bufPool[i].rs.extend(prefence)

    free_slots = list(range(POOL_N))
    slot_of = {}

    def palloc(name):
        i = free_slots.pop(0)
        slot_of[name] = i
        return pool[i], bufPool[i]

    def pfree(name):
        free_slots.append(slot_of.pop(name))

    def prename(new, old):
        slot_of[new] = slot_of.pop(old)

    grad_first = [True]

    def emit_stA(st):
        X, bX_ = palloc("X")
        for s in range(NSUB):
            dma("gpsimd", X[:, s * D:(s + 1) * D],
                x_d[st * ST + s * 128: st * ST + (s + 1) * 128, :],
                writes=(bX_,) if s == NSUB - 1 else (),
                extra=bX_.alldeps() if s == 0 else ())
        if ("x%d" % st) in tap_d:
            dma("sync", tap_d["x%d" % st][:, :], X[:, :], reads=(bX_,),
                writes=(buf("tap_x%d" % st),))
        LNX, bLNX = palloc("LNX")
        for s in range(NSUB):
            vts(X[:, s * D:(s + 1) * D], X[:, s * D:(s + 1) * D],
                svR[:, st * 4 + s:st * 4 + s + 1], svMR[:, st * 4 + s:st * 4 + s + 1],
                AL.mult, AL.add,
                reads=(bX_, bSvR, bSvMR), writes=(bX_,))
            ps, bps = next_ps("T")
            trans8(lambda c, s=s: X[:, s * D + c * 128: s * D + (c + 1) * 128],
                   ps, bps, reads=(bX_,))
            scopy(LNX.rearrange("p (c t) -> p c t", t=ST)[:, :, s * 128:(s + 1) * 128],
                  ps.rearrange("p (c t) -> p c t", t=128),
                  reads=(bps,), writes=(bLNX,) if s == NSUB - 1 else (),
                  extra=bLNX.alldeps() if s == 0 else ())
        pfree("X")

        def proj(wn, biasn, OUTT, bOUT, rhs, brhs):
            for oc in range(NCK):
                ps, bps = next_ps("M")
                mm_group([(ps[:, :],
                           Wsb[wn][:, kc * D + oc * 128: kc * D + (oc + 1) * 128],
                           rhs[:, kc * ST:(kc + 1) * ST], kc == 0, kc == NCK - 1)
                          for kc in range(NCK)],
                         reads=(bufW[wn], brhs), wpsum=bps)
                act(OUTT[:, oc * ST:(oc + 1) * ST], ps[:, :], AF.Identity,
                    bias=bias_sb[biasn][:, oc:oc + 1],
                    reads=(bps, bufBias[biasn]),
                    writes=(bOUT,) if oc == NCK - 1 else (),
                    extra=bOUT.alldeps() if oc == 0 else ())

        if st == 0 and "lnx" in tap_d:
            dma("sync", tap_d["lnx"][:, :], LNX[:, :], reads=(bLNX,),
                writes=(buf("tap_lnx"),))
        if ("lnx%d" % st) in tap_d:
            dma("sync", tap_d["lnx%d" % st][:, :], LNX[:, :], reads=(bLNX,),
                writes=(buf("tap_lnx%d" % st),))
        if st == 0 and "stats" in tap_d:
            dma("sync", tap_d["stats"][:, :], stat[:, :], reads=(bufStat,),
                writes=(buf("tap_stats"),))
        if ("bk%d" % st) in tap_d:
            dma("sync", tap_d["bk%d" % st][:, :], bias_sb["bk"][:, :],
                reads=(bufBias["bk"],), writes=(buf("tap_bk%d" % st),))
        if ("wk%d" % st) in tap_d:
            dma("sync", tap_d["wk%d" % st][:, :], Wsb["Wk"][:, :],
                reads=(bufW["Wk"],), writes=(buf("tap_wk%d" % st),))
        K, bK = palloc("K")
        proj("Wk", "bk", K, bK, LNX, bLNX)
        if st == 0 and "kfm" in tap_d:
            dma("sync", tap_d["kfm"][:, :], K[:, :], reads=(bK,),
                writes=(buf("tap_kfm"),))
        if ("kfm%d" % st) in tap_d:
            dma("sync", tap_d["kfm%d" % st][:, :], K[:, :], reads=(bK,),
                writes=(buf("tap_kfm%d" % st),))
        Q, bQ_ = palloc("Q")
        proj("Wq", "bq", Q, bQ_, LNX, bLNX)
        dma("sync", q_d[st * 128:(st + 1) * 128, :], Q[:, :],
            reads=(bQ_,), writes=(bufQd[st],))
        pfree("Q")
        bAlrS = buf("alrS")
        for s in range(NSUB):
            ps, bps = next_ps("M")
            mm_group([(ps[:, 0:1],
                       LNX[:, kc * ST + s * 128: kc * ST + (s + 1) * 128],
                       wlr_sb[:, kc:kc + 1], kc == 0, kc == NCK - 1)
                      for kc in range(NCK)],
                     reads=(bufSmall, bLNX), wpsum=bps)
            act(alrS[:, s:s + 1], ps[:, 0:1], AF.Sigmoid, bias=blr_bc[:, 0:1],
                reads=(bps, bufSmall), writes=(bAlrS,),
                extra=bAlrS.alldeps() if s == 0 else ())
        def falr(e):
            return e.tensor_scalar(out=alrneg[:, 0:NSUB], in0=alrS[:, 0:NSUB],
                                   scalar1=ALRF, scalar2=None, op0=AL.mult)
        sc.add("vector", falr, reads=(bAlrS,), writes=(bufAlr,), drain=True)
        Z1, bZ1 = palloc("Z1")
        S1, bS1 = palloc("S1")
        for oc in range(NCK):
            ps, bps = next_ps("M")
            mm_group([(ps[:, :],
                       Wsb["W1"][:, kc * D + oc * 128: kc * D + (oc + 1) * 128],
                       K[:, kc * ST:(kc + 1) * ST], kc == 0, kc == NCK - 1)
                      for kc in range(NCK)],
                     reads=(bufW["W1"], bK), wpsum=bps)
            act(Z1[:, oc * ST:(oc + 1) * ST], ps[:, :], AF.Identity,
                bias=bias_sb["b1"][:, oc:oc + 1], reads=(bps, bufBias["b1"]),
                writes=(bZ1,) if oc == NCK - 1 else (),
                extra=bZ1.alldeps() if oc == 0 else ())
            act(S1[:, oc * ST:(oc + 1) * ST], ps[:, :], AF.Sigmoid,
                bias=bias_sb["b1"][:, oc:oc + 1], reads=(bps, bufBias["b1"]),
                writes=(bS1,) if oc == NCK - 1 else (),
                extra=bS1.alldeps() if oc == 0 else ())
        A1, bA1 = palloc("A1")
        vtt(A1[:, :], Z1[:, :], S1[:, :], AL.mult, reads=(bZ1, bS1), writes=(bA1,))
        vtt(Z1[:, :], A1[:, :], S1[:, :], AL.mult, reads=(bA1, bS1), writes=(bZ1,))
        Wt, bWt = palloc("Wt")
        vtt(Wt[:, :], S1[:, :], A1[:, :], AL.add, reads=(bS1, bA1), writes=(bWt,))
        vstt(S1[:, :], Z1[:, :], -1.0, Wt[:, :], AL.mult, AL.add,
             reads=(bZ1, bWt), writes=(bS1,))
        SP1, bSP1 = S1, bS1
        prename("SP1", "S1")
        vtt(Wt[:, :], K[:, :], A1[:, :], AL.add, reads=(bK, bA1), writes=(bWt,))
        H1, bH1 = Wt, bWt
        prename("H1", "Wt")
        pfree("A1")
        for s in range(NSUB):
            ps, bps = next_ps("T")
            trans8(lambda c, s=s: K[:, c * ST + s * 128: c * ST + (s + 1) * 128],
                   ps, bps, reads=(bK,))
            scopy(ring["rk"][:, s * D:(s + 1) * D], ps[:, :], reads=(bps,),
                  writes=(bufRing["rk"],) if s == NSUB - 1 else (),
                  extra=bufRing["rk"].alldeps() if s == 0 else ())
        pfree("K")
        Z2, bZ2 = Z1, bZ1
        S2, bS2 = palloc("S2")
        for oc in range(NCK):
            ps, bps = next_ps("M")
            mm_group([(ps[:, :],
                       Wsb["W2"][:, kc * D + oc * 128: kc * D + (oc + 1) * 128],
                       H1[:, kc * ST:(kc + 1) * ST], kc == 0, kc == NCK - 1)
                      for kc in range(NCK)],
                     reads=(bufW["W2"], bH1), wpsum=bps)
            act(Z2[:, oc * ST:(oc + 1) * ST], ps[:, :], AF.Identity,
                bias=bias_sb["b2"][:, oc:oc + 1], reads=(bps, bufBias["b2"]),
                writes=(bZ2,) if oc == NCK - 1 else (),
                extra=bZ2.alldeps() if oc == 0 else ())
            act(S2[:, oc * ST:(oc + 1) * ST], ps[:, :], AF.Sigmoid,
                bias=bias_sb["b2"][:, oc:oc + 1], reads=(bps, bufBias["b2"]),
                writes=(bS2,) if oc == NCK - 1 else (),
                extra=bS2.alldeps() if oc == 0 else ())
        A2, bA2 = palloc("A2")
        vtt(A2[:, :], Z2[:, :], S2[:, :], AL.mult, reads=(bZ2, bS2), writes=(bA2,))
        vtt(Z2[:, :], A2[:, :], S2[:, :], AL.mult, reads=(bA2, bS2), writes=(bZ2,))
        W2t, bW2t = palloc("W2t")
        vtt(W2t[:, :], S2[:, :], A2[:, :], AL.add, reads=(bS2, bA2), writes=(bW2t,))
        vstt(S2[:, :], Z2[:, :], -1.0, W2t[:, :], AL.mult, AL.add,
             reads=(bZ2, bW2t), writes=(bS2,))
        SP2, bSP2 = S2, bS2
        prename("SP2", "S2")
        pfree("Z1")
        vtt(W2t[:, :], H1[:, :], A2[:, :], AL.add, reads=(bH1, bA2), writes=(bW2t,))
        PRED, bPRED = W2t, bW2t
        prename("PRED", "W2t")
        for s in range(NSUB):
            ps, bps = next_ps("T")
            trans8(lambda c, s=s: H1[:, c * ST + s * 128: c * ST + (s + 1) * 128],
                   ps, bps, reads=(bH1,))
            scopy(ring["rh1"][:, s * D:(s + 1) * D], ps[:, :], reads=(bps,),
                  writes=(bufRing["rh1"],) if s == NSUB - 1 else (),
                  extra=bufRing["rh1"].alldeps() if s == 0 else ())
        pfree("H1")
        RNEG, bRNEG = A2, bA2
        prename("RNEG", "A2")
        for oc in range(NCK):
            ps, bps = next_ps("M")
            mm_group([(ps[:, :],
                       Wsb["Wv"][:, kc * D + oc * 128: kc * D + (oc + 1) * 128],
                       LNX[:, kc * ST:(kc + 1) * ST], kc == 0, kc == NCK - 1)
                      for kc in range(NCK)],
                     reads=(bufW["Wv"], bLNX), wpsum=bps)
            vstt(RNEG[:, oc * ST:(oc + 1) * ST], ps[:, :],
                 bias_sb["bv"][:, oc:oc + 1], PRED[:, oc * ST:(oc + 1) * ST],
                 AL.add, AL.subtract,
                 reads=(bps, bufBias["bv"], bPRED),
                 writes=(bRNEG,) if oc == NCK - 1 else (),
                 extra=bRNEG.alldeps() if oc == 0 else ())
        if ("lnxlate%d" % st) in tap_d:
            dma("sync", tap_d["lnxlate%d" % st][:, :], LNX[:, :], reads=(bLNX,),
                writes=(buf("tap_lnxlate%d" % st),))
        pfree("LNX")
        pfree("PRED")
        E2, bE2 = palloc("E2")
        vtt(E2[:, :], RNEG[:, :], SP2[:, :], AL.mult, reads=(bRNEG, bSP2), writes=(bE2,))
        pfree("SP2")
        for s in range(NSUB):
            ps, bps = next_ps("T")
            trans8(lambda c, s=s: E2[:, c * ST + s * 128: c * ST + (s + 1) * 128],
                   ps, bps, reads=(bE2,))
            vts(ring["rdz2"][:, s * D:(s + 1) * D], ps[:, :], alrneg[:, s:s + 1],
                None, AL.mult, reads=(bps, bufAlr),
                writes=(bufRing["rdz2"],) if s == NSUB - 1 else (),
                extra=bufRing["rdz2"].alldeps() if s == 0 else ())
        T1, bT1 = palloc("T1")
        for ic in range(NCK):
            ps, bps = next_ps("M")
            mm_group([(ps[:, :],
                       Wsb["W2T"][:, kc * D + ic * 128: kc * D + (ic + 1) * 128],
                       E2[:, kc * ST:(kc + 1) * ST], kc == 0, kc == NCK - 1)
                      for kc in range(NCK)],
                     reads=(bufW["W2T"], bE2), wpsum=bps)
            vstt(T1[:, ic * ST:(ic + 1) * ST], ps[:, :], 0.0,
                 RNEG[:, ic * ST:(ic + 1) * ST], AL.add, AL.add,
                 reads=(bps, bRNEG), writes=(bT1,) if ic == NCK - 1 else (),
                 extra=bT1.alldeps() if ic == 0 else ())
        pfree("E2")
        pfree("RNEG")
        D1, bD1 = palloc("D1")
        vtt(D1[:, :], T1[:, :], SP1[:, :], AL.mult, reads=(bT1, bSP1), writes=(bD1,))
        pfree("T1")
        pfree("SP1")
        for s in range(NSUB):
            ps, bps = next_ps("T")
            trans8(lambda c, s=s: D1[:, c * ST + s * 128: c * ST + (s + 1) * 128],
                   ps, bps, reads=(bD1,))
            vts(ring["rdz1"][:, s * D:(s + 1) * D], ps[:, :], alrneg[:, s:s + 1],
                None, AL.mult, reads=(bps, bufAlr),
                writes=(bufRing["rdz1"],) if s == NSUB - 1 else (),
                extra=bufRing["rdz1"].alldeps() if s == 0 else ())
        pfree("D1")
        for rn in ("rk", "rh1", "rdz1", "rdz2"):
            if rn + str(st) in tap_d:
                dma("sync", tap_d[rn + str(st)][:, :], ring[rn][:, :],
                    reads=(bufRing[rn],), writes=(buf("tap_%s%d" % (rn, st)),))
        if st == 0:
            for rn in ("rk", "rh1", "rdz1", "rdz2"):
                if rn in tap_d:
                    dma("sync", tap_d[rn][:, :], ring[rn][:, :],
                        reads=(bufRing[rn],), writes=(buf("tap_" + rn),))
            if "alr" in tap_d:
                dma("sync", tap_d["alr"][:, :], alrneg[:, :], reads=(bufAlr,),
                    writes=(buf("tap_alr"),))
        first = grad_first[0]
        for m, (lhs, dz) in enumerate((("rk", "rdz1"), ("rh1", "rdz2"))):
            for ic in range(NCK):
                gi = ic % 2
                gp = psG[gi]
                bgp = (bufPsGh[2 * gi], bufPsGh[2 * gi + 1])
                items = []
                for s in range(NSUB):
                    for hf in range(2):
                        items.append((gp[:, hf * 512:(hf + 1) * 512],
                                      ring[lhs][:, s * D + ic * 128: s * D + (ic + 1) * 128],
                                      ring[dz][:, s * D + hf * 512: s * D + (hf + 1) * 512],
                                      s == 0, s == NSUB - 1))
                mm_group(items, reads=(bufRing[lhs], bufRing[dz]), wpsum=bgp)
                stg, bstg = (stgA, bufStgA) if gi == 0 else (stgB, bufStgB)
                reg = bufG[(m, ic)]
                ccreg = (cc_in if m == 0 else cc2_in)[:, ic * D:(ic + 1) * D]
                if first:
                    if gi == 0:
                        scopy(stg[:, :], gp[:, :], reads=bgp, writes=(bstg,))
                    else:
                        vcopy(stg[:, :], gp[:, :], reads=bgp, writes=(bstg,))
                else:
                    dma("gpsimd", stg[:, :], ccreg, reads=(reg,), writes=(bstg,))
                    vtt(stg[:, :], gp[:, :], stg[:, :], AL.add,
                        reads=bgp + (bstg,), writes=(bstg,))
                dma("sync", ccreg, stg[:, :], reads=(bstg,), writes=(reg,))
        gp = psG[0]
        bgp = (bufPsGh[0],)
        for m, dz in enumerate(("rdz1", "rdz2")):
            for ic in range(NCK):
                col = m * NCK + ic
                items = []
                for s in range(NSUB):
                    items.append((gp[:, col:col + 1],
                                  ring[dz][:, s * D + ic * 128: s * D + (ic + 1) * 128],
                                  ones_col[:, :], s == 0, s == NSUB - 1))
                mm_group(items, reads=(bufRing[dz], bufSmall), wpsum=bgp)
        if first:
            vcopy(stgA[:, 0:16], gp[:, 0:16], reads=bgp, writes=(bufStgA,),
                  drain=True)
        else:
            dma("gpsimd", stgA[:, 0:8], cc_in[:, 8192:8200],
                reads=(bufGb,), writes=(bufStgA,))
            dma("gpsimd", stgA[:, 8:16], cc2_in[:, 8192:8200],
                reads=(bufGb,), writes=(bufStgA,))
            vtt(stgA[:, 0:16], gp[:, 0:16], stgA[:, 0:16], AL.add,
                reads=bgp + (bufStgA,), writes=(bufStgA,), drain=True)
        dma("sync", cc_in[:, 8192:8200], stgA[:, 0:8], reads=(bufStgA,),
            writes=(bufGb,))
        dma("sync", cc2_in[:, 8192:8200], stgA[:, 8:16], reads=(bufStgA,),
            writes=(bufGb,))
        tname = "ccin%d" % st
        if tname in tap_d:
            ccdeps0 = []
            for kk in bufG.values():
                ccdeps0.extend(kk.w)
                kk.rs.append(sc.all[-1])
            ccdeps0 = []
            for kk in list(bufG.values()) + [bufGb]:
                ccdeps0.extend(kk.w)
            dma("sync", tap_d[tname][:, :], cc_in[:, :], extra=ccdeps0,
                reads=tuple(bufG.values()) + (bufGb,),
                writes=(buf("tap_" + tname),))
        grad_first[0] = False

    nstA = NST
    if variant.startswith("nst"):
        nstA = int(variant[3:4])
    for st in range(nstA):
        emit_stA(st)
        assert len(free_slots) == POOL_N, (st, free_slots, slot_of)

    ccdeps1 = []
    for (m, c), k in bufG.items():
        if m == 0:
            ccdeps1.extend(k.w)
    ccdeps1.extend(bufGb.w)
    ccdeps2 = []
    for (m, c), k in bufG.items():
        if m == 1:
            ccdeps2.extend(k.w)
    ccdeps2.extend(bufGb.w)
    bufCCout2 = buf("cc_out2")

    if "nocc" in variant:
        dma("gpsimd", cc_out[:, :], cc_in[:, :], extra=ccdeps1, writes=(bufCCout,))
        dma("gpsimd", cc2_out[:, :], cc2_in[:, :], extra=ccdeps2, writes=(bufCCout2,))
    else:
        def f_cc1(e):
            return e.collective_compute(
                "AllReduce", AL.add, ins=[cc_in[:, :]], outs=[cc_out[:, :]],
                replica_groups=[[0, 1, 2, 3, 4, 5, 6, 7]])
        op_cc = sc.add("gpsimd", f_cc1, extra=ccdeps1, writes=(bufCCout,))
        op_cc.sem = "CC"

        def f_cc2(e):
            return e.collective_compute(
                "AllReduce", AL.add, ins=[cc2_in[:, :]], outs=[cc2_out[:, :]],
                replica_groups=[[0, 1, 2, 3, 4, 5, 6, 7]])
        op_cc2 = sc.add("gpsimd", f_cc2, extra=ccdeps2, writes=(bufCCout2,))
        op_cc2.sem = "CC"

    def finish_early():
        if "ccread" in variant:
            dma("gpsimd", stgA[:, :], cc_out[:, 0:D], reads=(bufCCout,),
                writes=(bufStgA,))
        dma("sync", out_d[0:128, :], stgA[:, :], reads=(bufStgA,),
            writes=(buf("outearly"),))
        sc.add("gpsimd", lambda e: e.memset(ident[0:1, 0:1], 0.0), extra=buf("outearly").w)
        import contextlib
        stack = contextlib.ExitStack()
        sems = {s: stack.enter_context(nc.semaphore("sem_" + s))
                for s in ("T", "V", "S", "G", "Y", "DG", "DY", "CC")}
        sc.realize(nc, sems)
        stack.close()
        return nc

    if "cut1" in variant:
        return finish_early()

    fence5 = bufStgA.alldeps() + bufStgB.alldeps()
    prep_i = [0]

    def prep_piece(src_ap, dst_ap3, wbuf):
        i = prep_i[0] % 4
        prep_i[0] += 1
        stg, bstg = stgW[i], bufStgW[i]
        dma("gpsimd", stg[:, :], src_ap, writes=(bstg,),
            extra=fence5 + bstg.alldeps())
        ps, bps = next_ps("T")
        trans8(lambda b: stg[:, b * 128:(b + 1) * 128], ps, bps, reads=(bstg,))
        i2 = prep_i[0] % 4
        prep_i[0] += 1
        stg2, bstg2 = stgW[i2], bufStgW[i2]
        scopy(stg2[:, :], ps[:, :], reads=(bps,), writes=(bstg2,),
              extra=fence5 + bstg2.alldeps())
        dma("sync", dst_ap3, stg2.rearrange("p (b t) -> p b t", t=128),
            reads=(bstg2,), writes=(wbuf,))

    for wt_d, w_src, wbn in ((wgT_d, wg_d, "wgT"), (wuT_d, wu_d, "wuT")):
        for hc in range(NHC):
            dst = wt_d.rearrange("p (dc h) -> p dc h", h=H)[:, :, hc * 128:(hc + 1) * 128]
            prep_piece(w_src[hc * 128:(hc + 1) * 128, :], dst, bufWT_d[wbn])
    for dc in range(NCK):
        for ss in range(3):
            dst = wdT_d.rearrange("p (hc d) -> p hc d", d=D)[
                :, ss * 8:(ss + 1) * 8, dc * 128:(dc + 1) * 128]
            prep_piece(wd_d[dc * 128:(dc + 1) * 128, ss * 1024:(ss + 1) * 1024],
                       dst, bufWT_d["wdT"])

    def r5dep():
        dd = []
        for b in bufStgW:
            dd.extend(b.alldeps())
        dd.extend(bufStgP.alldeps())
        dd.extend(bufStgPb.alldeps())
        dd.extend(fence5)
        return dd

    fenceP2 = []
    for i in range(4):
        fenceP2.extend(bufPool[i].alldeps())

    stgQ = sb("stgQ", [128, D], F32, "RW", 8192)
    stgQb = sb("stgQb", [128, D], F32, "RW", 12288)
    bufStgQ, bufStgQb = buf("stgQ"), buf("stgQb")

    def upd_chunk(wn, nwt, bcc, c, ccol, bufs, ccsrc):
        (sg, bsg), (sgb, bsgb) = bufs
        ex = r5dep() if c < 2 else []
        for b_ in (bufStgQ, bufStgQb):
            if c < 2:
                ex = ex + b_.alldeps()
        dma("gpsimd", sg[:, :], ccsrc[:, ccol:ccol + D],
            reads=(bcc,), writes=(bufStgP if sg is stgP else bufStgQ,), extra=ex)
        act(sgb[:, :], sg[:, :], AF.Abs,
            reads=(bufStgP if sg is stgP else bufStgQ,),
            writes=(bsgb,), extra=ex)
        vts(sgb[:, :], sgb[:, :], EPS, None, AL.add, reads=(bsgb,), writes=(bsgb,))
        vrecip(sgb[:, :], sgb[:, :], reads=(bsgb,), writes=(bsgb,))
        vtt(sgb[:, :], sgb[:, :], sg[:, :], AL.mult,
            reads=(bsgb, bufStgP if sg is stgP else bufStgQ), writes=(bsgb,))
        vts(sgb[:, :], sgb[:, :], -LR, None, AL.mult, reads=(bsgb,), writes=(bsgb,))
        vstt(nwt[:, c * D:(c + 1) * D], Wsb[wn][:, c * D:(c + 1) * D],
             1.0 - LR * WD, sgb[:, :], AL.mult, AL.add,
             reads=(bufW[wn], bsgb),
             writes=(bnW[wn],) if c == NCK - 1 else (),
             extra=fenceP2 if c == 0 else ())

    for m, wn in enumerate(("W1", "W2")):
        nwt = nW1 if m == 0 else nW2
        bcc = bufCCout if m == 0 else bufCCout2
        ccsrc = cc_out if m == 0 else cc2_out
        for c in range(NCK):
            ccol = c * D
            bufs = (((stgP, bufStgP), (stgPb, bufStgPb)) if c % 2 == 0
                    else ((stgQ, bufStgQ), (stgQb, bufStgQb)))
            upd_chunk(wn, nwt, bcc, c, ccol, bufs, ccsrc)
    for m in range(2):
        nbn = "nb%d" % (m + 1)
        bn = "b%d" % (m + 1)
        bga, bgb = buf("gA"), buf("gB")
        dma("gpsimd", gA[:, :], (cc_out if m == 0 else cc2_out)[:, 8192:8200],
            reads=(bufCCout if m == 0 else bufCCout2,), writes=(bga,))
        act(gB[:, :], gA[:, :], AF.Abs, reads=(bga,), writes=(bgb,), drain=True)
        vts(gB[:, :], gB[:, :], EPS, None, AL.add, reads=(bgb,), writes=(bgb,),
            drain=True)
        vrecip(gB[:, :], gB[:, :], reads=(bgb,), writes=(bgb,), drain=True)
        vtt(gB[:, :], gB[:, :], gA[:, :], AL.mult, reads=(bgb, bga), writes=(bgb,),
            drain=True)
        vts(gB[:, :], gB[:, :], -LR, None, AL.mult, reads=(bgb,), writes=(bgb,),
            drain=True)
        def fstt(e, nbn=nbn, bn=bn):
            return e.scalar_tensor_tensor(out=bias_sb[nbn][:, :],
                                          in0=bias_sb[bn][:, :],
                                          scalar=1.0 - LR * WD, in1=gB[:, :],
                                          op0=AL.mult, op1=AL.add)
        sc.add("vector", fstt, reads=(bufBias[bn], bgb),
               writes=(bufBias[nbn],), drain=True)

    if "grads" in tap_d:
        dma("sync", tap_d["grads"][:, :], cc_out[:, :], reads=(bufCCout,),
            writes=(buf("tap_g"),))
    if "nW1" in tap_d:
        dma("sync", tap_d["nW1"][:, :], nW1[:, :], reads=(bnW["W1"],),
            writes=(buf("tap_nW1"),))

    if "cut2" in variant:
        return finish_early()

    fence3 = []
    for b in bufPool:
        fence3.extend(b.alldeps())

    def emit_stB(st):
        f3 = fence3 if st == 0 else ()
        dma("sync", bQ[:, :], q_d[st * 128:(st + 1) * 128, :],
            reads=(bufQd[st],), writes=(bufQ,), extra=f3)
        for oc in range(NCK):
            ps, bps = next_ps("M")
            mm_group([(ps[:, :], nW1[:, kc * D + oc * 128: kc * D + (oc + 1) * 128],
                       bQ[:, kc * ST:(kc + 1) * ST], kc == 0, kc == NCK - 1)
                      for kc in range(NCK)],
                     reads=(bnW["W1"], bufQ), wpsum=bps, extra=f3)
            act(bTA[:, oc * ST:(oc + 1) * ST], ps[:, :], AF.Silu,
                bias=bias_sb["nb1"][:, oc:oc + 1], reads=(bps, bufBias["nb1"]),
                writes=(bufTA,) if oc == NCK - 1 else (),
                extra=(list(f3) + bufTA.alldeps()) if oc == 0 else ())
        vtt(bQH1[:, :], bQ[:, :], bTA[:, :], AL.add, reads=(bufQ, bufTA),
            writes=(bufQH1,), extra=f3)
        for oc in range(NCK):
            ps, bps = next_ps("M")
            mm_group([(ps[:, :], nW2[:, kc * D + oc * 128: kc * D + (oc + 1) * 128],
                       bQH1[:, kc * ST:(kc + 1) * ST], kc == 0, kc == NCK - 1)
                      for kc in range(NCK)],
                     reads=(bnW["W2"], bufQH1), wpsum=bps)
            act(bTA[:, oc * ST:(oc + 1) * ST], ps[:, :], AF.Silu,
                bias=bias_sb["nb2"][:, oc:oc + 1], reads=(bps, bufBias["nb2"]),
                writes=(bufTA,) if oc == NCK - 1 else (),
                extra=bufTA.alldeps() if oc == 0 else ())
        vtt(bQ[:, :], bQH1[:, :], bTA[:, :], AL.add, reads=(bufQH1, bufTA),
            writes=(bufQ,))
        if "retr" in tap_d and st == 0:
            dma("sync", tap_d["retr"][:, :], bQ[:, :], reads=(bufQ,),
                writes=(buf("tap_r"),))
        for s in range(NSUB):
            ps, bps = next_ps("T")
            trans8(lambda c, s=s: bQ[:, c * ST + s * 128: c * ST + (s + 1) * 128],
                   ps, bps, reads=(bufQ,))
            act(bRT[:, s * D:(s + 1) * D], ps[:, :], AF.Identity,
                accum=stat[:, s:s + 1], reads=(bps,),
                writes=((bufRT,) if s == NSUB - 1 else ()) + (bufStat,),
                extra=(list(f3) + bufRT.alldeps()
                       + (bufStgA.alldeps() + bufStgB.alldeps()
                          + bufStgP.alldeps() + bufStgPb.alldeps()
                          + [d for b_ in bufStgW for d in b_.alldeps()])
                       if st == 0 else []) if s == 0 else ())
            act(bTA[:, s * D:(s + 1) * D], bRT[:, s * D:(s + 1) * D], AF.Square,
                accum=stat[:, 4 + s:5 + s], reads=(bufRT,), writes=(bufTA, bufStat))
        ln_minimath()
        for s in range(NSUB):
            dma("gpsimd", bQH1[:, s * D:(s + 1) * D],
                x_d[st * ST + s * 128: st * ST + (s + 1) * 128, :],
                writes=(bufQH1,) if s == NSUB - 1 else (),
                extra=(bufQH1.alldeps() + list(f3)) if s == 0 else ())
        for s in range(NSUB):
            vts(bQH1[:, s * D:(s + 1) * D], bQH1[:, s * D:(s + 1) * D],
                stat[:, 448 + s:449 + s], None, AL.add,
                reads=(bufQH1, bufStat), writes=(bufQH1,))
            vstt(bTA[:, s * D:(s + 1) * D], bRT[:, s * D:(s + 1) * D],
                 stat[:, 384 + s:385 + s], bQH1[:, s * D:(s + 1) * D], AL.mult, AL.add,
                 reads=(bufRT, bufQH1, bufStat),
                 writes=(bufTA,) if s == NSUB - 1 else (),
                 extra=bufTA.alldeps() if s == 0 else ())
        dma("sync", xp_d[st * ST:(st + 1) * ST, :].rearrange("(s p) d -> p s d", p=128),
            bTA.rearrange("p (s d) -> p s d", d=D), reads=(bufTA,), writes=(bufXpd[st],))
        if "xp" in tap_d and st == 0:
            dma("sync", tap_d["xp"][:, :], bTA[:, :], reads=(bufTA,),
                writes=(buf("tap_xp"),))

    for st in range(NST):
        emit_stB(st)

    if "cut3" in variant:
        return finish_early()

    fR1 = []
    for n in WNAMES:
        fR1.extend(bufW[n].alldeps())
    fR2 = []
    for b in bufRing.values():
        fR2.extend(b.alldeps())
    fR2.extend(r5dep())
    dma("sync", wgT[:, :], wgT_d[:, :], reads=(bufWT_d["wgT"],),
        writes=(bufFW["wgT"],), extra=fR1)
    dma("sync", wuT[:, :], wuT_d[:, :], reads=(bufWT_d["wuT"],),
        writes=(bufFW["wuT"],), extra=fR1)
    dma("sync", wdTa[:, :], wdT_d[:, 0:16 * D], reads=(bufWT_d["wdT"],),
        writes=(bufFW["wdTa"],), extra=fR2)
    fRW = (bufStgA.alldeps() + bufStgB.alldeps() + bufStgP.alldeps()
           + bufStgPb.alldeps() + bufRT.alldeps())
    for b_ in bufStgW:
        fRW.extend(b_.alldeps())
    dma("sync", wdTb[:, :], wdT_d[:, 16 * D:24 * D], reads=(bufWT_d["wdT"],),
        writes=(bufFW["wdTb"],), extra=fRW)

    fenceB = []
    for b in (bufQ, bufTA, bufQH1, bufRT, bnW["W1"], bnW["W2"]):
        fenceB.extend(b.alldeps())

    pool_mm[0] = [(psM[0], bufPsM[0]), (psM[1], bufPsM[1])]

    def emit_stF(st):
        fB = fenceB if st == 0 else ()
        dma("sync", fXP.rearrange("p (s d) -> p s d", d=D),
            xp_d[st * ST:(st + 1) * ST, :].rearrange("(s p) d -> p s d", p=128),
            reads=(bufXpd[st],), writes=(bufFXP,), extra=list(fB) + bufFXP.alldeps())
        ln_stats(fXP, bufFXP, fLN_tm, bufFLNtm,
                 extra0=list(fB) + bufFLNtm.alldeps())
        for s in range(NSUB):
            vts(fLN_tm[:, s * D:(s + 1) * D], fXP[:, s * D:(s + 1) * D],
                stat[:, 384 + s:385 + s], stat[:, 448 + s:449 + s], AL.mult, AL.add,
                reads=(bufFXP, bufStat), writes=(bufFLNtm,),
                extra=list(fB) if s == 0 else ())
            ps, bps = next_ps("T")
            trans8(lambda c, s=s: fLN_tm[:, s * D + c * 128: s * D + (c + 1) * 128],
                   ps, bps, reads=(bufFLNtm,))
            scopy(fLN_fm.rearrange("p (c t) -> p c t", t=ST)[:, :, s * 128:(s + 1) * 128],
                  ps.rearrange("p (c t) -> p c t", t=128), reads=(bps,),
                  writes=(bufFLNfm,) if s == NSUB - 1 else (),
                  extra=(list(fB) + bufFLNfm.alldeps()) if s == 0 else ())
        ffin_dead = bufFFIN.alldeps()
        for hc in range(NHC):
            psg, bpsg = next_ps("M")
            mm_group([(psg[:, :], wgT[:, kc * H + hc * 128: kc * H + (hc + 1) * 128],
                       fLN_fm[:, kc * ST:(kc + 1) * ST], kc == 0, kc == NCK - 1)
                      for kc in range(NCK)],
                     reads=(bufFW["wgT"], bufFLNfm), wpsum=bpsg)
            psu, bpsu = next_ps("M")
            mm_group([(psu[:, :], wuT[:, kc * H + hc * 128: kc * H + (hc + 1) * 128],
                       fLN_fm[:, kc * ST:(kc + 1) * ST], kc == 0, kc == NCK - 1)
                      for kc in range(NCK)],
                     reads=(bufFW["wuT"], bufFLNfm), wpsum=bpsu)
            act(fH[:, hc * ST:(hc + 1) * ST], psg[:, :], AF.Silu, scale=1.0 / 127.0,
                reads=(bpsg,), writes=(bufFH,),
                extra=ffin_dead if hc == 0 else ())

            def fmul(e, hc=hc, psu=psu):
                return e.tensor_tensor(out=fH[:, hc * ST:(hc + 1) * ST],
                                       in0=fH[:, hc * ST:(hc + 1) * ST],
                                       in1=psu[:, :], op=AL.mult)
            sc.add("vector", fmul, reads=(bufFH, bpsu), writes=(bufFH,))
        for s in range(NSUB):
            gi = s % 2
            gp = psG[gi]
            bgp = (bufPsGh[2 * gi], bufPsGh[2 * gi + 1])
            items = []
            for hc in range(NHC):
                for hf in range(2):
                    items.append((gp[:, hf * 512:(hf + 1) * 512],
                                  fH[:, hc * ST + s * 128: hc * ST + (s + 1) * 128],
                                  wd_ap(hc, hf),
                                  hc == 0, hc == NHC - 1))
            mm_group(items, reads=(bufFH, bufFW["wdTa"], bufFW["wdTb"]), wpsum=bgp)
            act(fOUT[:, s * D:(s + 1) * D], gp[:, :], AF.Identity,
                accum=stat[:, s:s + 1], reads=bgp, writes=(bufFOUT, bufStat))
            act(fLN_tm[:, s * D:(s + 1) * D], fOUT[:, s * D:(s + 1) * D], AF.Square,
                accum=stat[:, 4 + s:5 + s], reads=(bufFOUT,),
                writes=(bufFLNtm, bufStat))
        ln_minimath()
        fh_dead = bufFH.alldeps()
        for s in range(NSUB):
            vstt(fFIN[:, s * D:(s + 1) * D], fOUT[:, s * D:(s + 1) * D],
                 stat[:, 384 + s:385 + s], fXP[:, s * D:(s + 1) * D], AL.mult, AL.add,
                 reads=(bufFOUT, bufFXP, bufStat), writes=(bufFFIN,),
                 extra=fh_dead if s == 0 else ())
            vts(fFIN[:, s * D:(s + 1) * D], fFIN[:, s * D:(s + 1) * D],
                stat[:, 448 + s:449 + s], None, AL.add,
                reads=(bufFFIN, bufStat), writes=(bufFFIN,))
            dma("sync", out_d[st * ST + s * 128: st * ST + (s + 1) * 128, :],
                fFIN[:, s * D:(s + 1) * D], reads=(bufFFIN,),
                writes=(buf("out%d_%d" % (st, s)),))

    for st in range(NST):
        emit_stF(st)

    outdeps = []
    for st in range(NST):
        for s in range(NSUB):
            outdeps.extend(buf("out%d_%d" % (st, s)).w)
    sc.add("gpsimd", lambda e: e.memset(ident[0:1, 0:1], 0.0), extra=outdeps)

    import contextlib
    stack = contextlib.ExitStack()
    sems = {s: stack.enter_context(nc.semaphore("sem_" + s)) for s in SEMNAMES}
    sc.realize(nc, sems)
    stack.close()
    return nc


_NC_CACHE = {}


def get_nc(taps=()):
    key = tuple(taps)
    if key not in _NC_CACHE:
        _NC_CACHE[key] = build_nc(taps)
    return _NC_CACHE[key]


def kernel(**inputs):
    nc = get_nc()
    full_x = np.ascontiguousarray(np.asarray(inputs["L0_attn_x"], dtype=np.float32))
    assert full_x.shape == (8, 4096, 1024)
    base = {}
    for n in ("L0_attn_Wk", "L0_attn_bk", "L0_attn_Wq", "L0_attn_bq", "L0_attn_Wv",
              "L0_attn_bv", "L0_attn_Wlr", "L0_attn_blr", "L0_attn_W1", "L0_attn_b1",
              "L0_attn_W2", "L0_attn_b2"):
        base[n] = np.ascontiguousarray(np.asarray(inputs[n], dtype=np.float32))
    for n in ("L0_ffn_w_gate", "L0_ffn_w_up", "L0_ffn_w_down"):
        base[n] = np.ascontiguousarray(np.asarray(inputs[n], dtype=np.int32))
    in_maps = [dict(base, L0_attn_x=full_x[c]) for c in range(8)]
    res = run_bass_kernel_spmd(nc, in_maps, core_ids=list(range(8)))
    out = np.stack([res.results[c]["out"] for c in range(8)], axis=0)
    return np.ascontiguousarray(out.astype(np.float32))
